# version 01
# baseline (speedup 1.0000x reference)
import numpy as np
import ml_dtypes
from contextlib import ExitStack

import concourse.bass as bass
import concourse.mybir as mybir

F32 = mybir.dt.float32
BF16 = mybir.dt.bfloat16
AF = mybir.ActivationFunctionType
ALU = mybir.AluOpType

D = 1024; H = 16; DD = 64; FF = 4096; IG = 2816
L = 128; SC = 512
NDC = D // 128; NGC = IG // 128; NFC = FF // 128
NSUB = SC // L
CHAIN1 = [2048, 1024, 512, 256, 128, 64, 32, 16, 8, 4, 2, 1]
CHAIN3 = [32, 16, 8, 4, 2, 1]
CHAIN4 = [2048, 1024, 512, 256, 128, 64, 32, 16, 8, 4, 2, 1]


class Tile:
    __slots__ = ("name", "writer", "readers")
    def __init__(self, name):
        self.name = name; self.writer = None; self.readers = []


class B:
    ENGS = ("vector", "scalar", "tensor", "sync", "gp")

    def __init__(self):
        self.q = []
        self.idx = {e: 0 for e in self.ENGS}
        self.tiles = {}

    def T(self, name):
        if name not in self.tiles:
            self.tiles[name] = Tile(name)
        return self.tiles[name]

    def op(self, eng, emit, reads=(), writes=(), inc=None, force_inc=False):
        if inc is None:
            inc = 16 if eng in ("sync", "gp") else 1
        reads = [self.T(r) for r in reads]
        writes = [self.T(w) for w in writes]
        waits = []
        for t in reads:
            if t.writer is not None:
                waits.append(t.writer)
        for t in writes:
            if t.writer is not None:
                waits.append(t.writer)
            waits.extend(t.readers)
        tok = (eng, self.idx[eng])
        self.idx[eng] += 1
        self.q.append((eng, emit, list(set(waits)), inc, force_inc))
        for t in reads:
            t.readers.append(tok)
        for t in writes:
            t.writer = tok
            t.readers = []
        return tok

    def v(self, emit, reads=(), writes=()):
        return self.op("vector", emit, reads, writes)
    def s(self, emit, reads=(), writes=()):
        return self.op("scalar", emit, reads, writes)
    def t(self, emit, reads=(), writes=()):
        return self.op("tensor", emit, reads, writes)
    def dl(self, emit, reads=(), writes=(), force_inc=False):
        return self.op("sync", emit, reads, writes, force_inc=force_inc)
    def dg(self, emit, reads=(), writes=(), force_inc=False):
        return self.op("gp", emit, reads, writes, force_inc=force_inc)

    def emit_into(self, block, sems, limit=None):
        q = self.q if limit is None else self.q[:limit]
        needed = set()
        for (_, _, waits, _, _) in q:
            needed.update(waits)
        import os as _os4
        dual = {"vector", "tensor", "scalar"} if _os4.environ.get("KDUAL") else set()
        val = {}
        cum = {}
        per_eng = {e: [] for e in self.ENGS}
        idx_in_eng = {e: 0 for e in self.ENGS}
        nth = {e: 0 for e in self.ENGS}
        for (eng, emit, waits, inc, force) in q:
            tok = (eng, idx_in_eng[eng]); idx_in_eng[eng] += 1
            do_inc = force or (tok in needed) or eng in ("sync", "gp") or bool(int(_os4.environ.get("KALLINC", "0")))
            sname = eng
            if do_inc:
                if eng in dual:
                    sname = eng + str(nth[eng] % 2)
                    nth[eng] += 1
                cum[sname] = cum.get(sname, 0) + inc
            else:
                sname = (eng + str((nth[eng] - 1) % 2)) if (eng in dual and nth[eng]) else eng
            val[tok] = (sname, cum.get(sname, 0))
            per_eng[eng].append((emit, waits, inc if do_inc else 0,
                                 (eng + str(nth[eng] % 2 if False else (nth[eng] - 1) % 2)) if False else None))
        waited_vals = {e: set() for e in self.ENGS}
        for (_, _, waits, _, _) in q:
            for t in waits:
                sn, vv = val[t]
                if sn == t[0]:
                    waited_vals[t[0]].add(vv)
        waited_vals = {e: sorted(v) for e, v in waited_vals.items()}
        attr = {"vector": "vector", "scalar": "scalar", "tensor": "tensor",
                "sync": "sync", "gp": "gpsimd"}
        import bisect
        for ename in self.ENGS:
            items = per_eng[ename]
            is_dma_q = ename in ("sync", "gp")
            wl = waited_vals[ename]
            def make(items, ename, is_dma_q, wl):
                def run(engobj):
                    lw = {}
                    protected = 0
                    cnt = 0
                    ni = 0
                    for (emit, waits, inc, _unused) in items:
                        if is_dma_q and wl and inc:
                            i = bisect.bisect_right(wl, cnt) - 1
                            if i >= 0 and wl[i] > protected:
                                engobj.wait_ge(sems[ename], wl[i])
                                protected = wl[i]
                        wm = {}
                        for t in waits:
                            sn, vv = val[t]
                            wm[sn] = max(wm.get(sn, 0), vv)
                        for (sn, v2) in sorted(wm.items()):
                            if lw.get(sn, 0) < v2:
                                engobj.wait_ge(sems[sn], v2)
                                lw[sn] = v2
                        inst = emit(engobj)
                        if inc:
                            if ename in dual:
                                inst.then_inc(sems[ename + str(ni % 2)], inc)
                                ni += 1
                            else:
                                inst.then_inc(sems[ename], inc)
                        cnt += inc
                return run
            getattr(block, attr[ename])(make(items, ename, is_dma_q, wl))


def build_program(TOK=2048, PRE=6144, FT=256, debug=(), limit=None):
    assert TOK % SC == 0 and PRE % (3 * SC) == 0 or PRE == 0
    nc = bass.Bass("TRN2", target_bir_lowering=False)
    NSC = TOK // SC
    NFT = TOK // FT
    QUART = PRE // 3 if PRE else 0

    din = {}
    def dram_in(name, shape, dt=F32):
        din[name] = nc.dram_tensor(name, list(shape), dt, kind="ExternalInput").ap()
        return din[name]

    xb = dram_in("xb", [D, PRE + TOK])
    dram_in("onehot", [128, 4])
    dram_in("tri", [L, L])
    dram_in("maskE", [L, L])
    dram_in("ident32", [128, 128])
    dram_in("wq", [D, D]); dram_in("wk", [D, D]); dram_in("wv", [D, D]); dram_in("wo", [D, D])
    dram_in("wif", [D, 2 * H])
    dram_in("bif_rep", [128, 2 * H])
    dram_in("ln1", [128, NDC]); dram_in("ln2", [128, NDC])
    dram_in("wg", [NGC, D, 128]); dram_in("wu", [NGC, D, 128]); dram_in("wd", [NDC, IG, 128])
    dram_in("w1", [NFC, D, 128]); dram_in("b1", [128, NFC])
    dram_in("w2", [NDC, FF, 128]); dram_in("b2", [128, NDC])
    out_t = nc.dram_tensor("out", [D, TOK], F32, kind="ExternalOutput").ap()
    dbg_t = {}
    for name, shape in debug:
        dbg_t[name] = nc.dram_tensor("dbg_" + name, list(shape), F32, kind="ExternalOutput").ap()

    b = B()
    st = ExitStack()
    def sb(name, shape, dt=F32):
        return st.enter_context(nc.sbuf_tensor(name, list(shape), dt))
    def psum(name, shape):
        return st.enter_context(nc.psum_tensor(name, list(shape), F32))

    ST = sb("stream", [128, NDC * TOK])
    HSC = sb("hsc", [128, NDC * SC], BF16)
    AR3 = sb("ar3", [128, NDC * SC + 2 * FT])
    RREP = sb("rrep", [128, 2 * SC])
    AR4 = sb("ar4", [128, 2 * NDC * SC], BF16)
    AR5 = sb("ar5", [128, 4 * NDC * SC], BF16)
    WST = sb("wst", [128, NDC * D], BF16)
    WIF = sb("wif_s", [128, NDC * 2 * H], BF16)
    BIFR = sb("bifr_s", [128, 2 * H])
    LN1 = sb("ln1_s", [128, NDC]); LN2 = sb("ln2_s", [128, NDC])
    B1S = sb("b1s", [128, NFC]); B2S = sb("b2s", [128, NDC])
    UU = sb("uu", [128, 8 * FT])
    GZR = sb("gzr", [128, NSUB * 2 * H])
    ZETA = sb("zeta", [128, NSUB * H])
    LFT = sb("lft", [128, NSUB * H])
    NLFT = sb("nlft", [128, NSUB * H])
    CUMT = sb("cumt", [128, H])
    CROWS = sb("crows", [65, H * L])
    CUMROWS = CROWS[0:1, :]
    BEXPROWS = CROWS[64:65, :]
    CUML = sb("cuml", [1, H])
    WTILD = sb("wtild", [128, H])
    WTB = sb("wtb", [128, H], BF16)
    D0T = sb("d0t", [128, H])
    LAMR = sb("lamr", [128, H])
    APRE = sb("apre", [128, 8 * 128])
    YSCR = sb("yscr", [128, 8])
    ABF = sb("abf", [128, 8 * 128], BF16)
    DNT = sb("dnt", [128, 2 * H])
    SCC = sb("scc", [64, H * DD]); SCN = sb("scn", [64, H])
    SCCB = sb("sccb", [128, H * DD], BF16); SCNB = sb("scnb", [128, H], BF16)
    SNAP = sb("snap", [64, 2 * H * (DD + 1)], BF16)
    YT = sb("yt", [128, D])
    YF = sb("yf", [128, NDC * 128], BF16)
    OH = sb("oh", [128, 4])
    TRI = sb("tri_s", [L, L]); MASKE = sb("maske_s", [L, L])
    ID32 = sb("id32", [128, 128]); IDB = sb("idb", [128, 128], BF16)
    ONESC = sb("onesc", [128, 1], BF16)
    ONESC32 = sb("onesc32", [128, 1])
    ONESR = sb("onesr", [65, 128])
    RAB = sb("rab", [65, SC])
    ROWA = RAB[0:1, :]; ROWB = RAB[64:65, :]
    CFB = sb("cfb", [16, 128])
    CONSTS = sb("consts", [128, 5])

    PB = [psum(f"pb{i}", [128, 512]) for i in range(8)]

    b.dl(lambda e: e.dma_start(out=OH[:], in_=din["onehot"]), writes=["OH"])
    b.dl(lambda e: e.dma_start(out=TRI[:], in_=din["tri"]), writes=["TRI"])
    b.dl(lambda e: e.dma_start(out=MASKE[:], in_=din["maskE"]), writes=["MASKE"])
    b.dl(lambda e: e.dma_start(out=ID32[:], in_=din["ident32"]), writes=["ID32"])
    b.dg(lambda e: e.dma_start(out=IDB[:], in_=din["ident32"]), writes=["IDB"])
    b.dl(lambda e: e.dma_start(out=BIFR[:], in_=din["bif_rep"]), writes=["BIFR"])
    b.dl(lambda e: e.dma_start(out=LN1[:], in_=din["ln1"]), writes=["LN1"])
    b.dl(lambda e: e.dma_start(out=LN2[:], in_=din["ln2"]), writes=["LN2"])
    b.dl(lambda e: e.dma_start(out=B1S[:], in_=din["b1"]), writes=["B1S"])
    b.dl(lambda e: e.dma_start(out=B2S[:], in_=din["b2"]), writes=["B2S"])
    b.v(lambda e: e.tensor_scalar(out=B1S[:], in0=B1S[:], scalar1=256.0, scalar2=1088.5,
                                  op0=ALU.mult, op1=ALU.add), reads=["B1S"], writes=["B1S"])
    b.v(lambda e: e.tensor_scalar(out=B2S[:], in0=B2S[:], scalar1=256.0, scalar2=2048.5,
                                  op0=ALU.mult, op1=ALU.add), reads=["B2S"], writes=["B2S"])
    for c in range(NDC):
        b.dg(lambda e, c=c: e.dma_start(out=WIF[:, c * 2 * H:(c + 1) * 2 * H],
                                        in_=din["wif"][c * 128:(c + 1) * 128, :]), writes=["WIF"])
    b.v(lambda e: e.memset(ONESC[:], 1.0), writes=["ONESC"])
    b.v(lambda e: e.memset(ONESC32[:], 1.0), writes=["ONESC32"])
    b.v(lambda e: e.memset(ONESR[:, :], 1.0), writes=["ONESR"])
    for i, val in enumerate([1e-5, 1e-6, -4.25, -4.0, 1e-11]):
        b.v(lambda e, i=i, val=val: e.memset(CONSTS[:, i:i + 1], val), writes=["CONSTS"])
        nc.const_aps.register(F32, val, CONSTS[:, i:i + 1])
    for _p in range(getattr(b, "_pad", 0)):
        b.v(lambda e, _p=_p: e.memset(YSCR[0:1, 4 + _p % 4:5 + _p % 4], 0.0), writes=[f"_pad{_p}"])
    b.v(lambda e: e.memset(SCC[:], 0.0), writes=["SCC"])
    b.v(lambda e: e.memset(SCN[:], 0.0), writes=["SCN"])

    XS = lambda c: AR3[:, c * SC:(c + 1) * SC]
    GAR = AR3[:, NDC * SC:NDC * SC + FT]
    SIL = AR3[:, NDC * SC:NDC * SC + SC]
    GBR = AR3[:, NDC * SC + FT:NDC * SC + 2 * FT]
    XSB = lambda c: AR4[:, c * SC:(c + 1) * SC]
    XSQ = lambda c: AR4[:, NDC * SC + c * SC:NDC * SC + (c + 1) * SC]
    QF = lambda c: AR5[:, c * SC:(c + 1) * SC]
    KFt = lambda c: AR5[:, NDC * SC + c * SC:NDC * SC + (c + 1) * SC]
    KTOK = lambda sub: AR5[:, 2 * NDC * SC + sub * D: 2 * NDC * SC + (sub + 1) * D]
    VTOK = lambda sub: AR5[:, 3 * NDC * SC + sub * D: 3 * NDC * SC + (sub + 1) * D]
    VWT = sb("vwt", [128, D], BF16)
    HQc = lambda fc: AR5[:, fc * FT:(fc + 1) * FT]
    SWTc = lambda g: AR5[:, g * SC:(g + 1) * SC]
    WCH = lambda slot: WST[:, slot * NDC * 128:(slot + 1) * NDC * 128]
    A2 = lambda c: WST[:, 2 * D + c * FT: 2 * D + (c + 1) * FT]
    U = lambda i: UU[:, i * FT:(i + 1) * FT]
    FOUT = UU[:, 5 * FT:6 * FT]

    def ln_finalize(n, rms_only=False):
        b.v(lambda e: e.tensor_scalar_mul(ROWA[0:1, 0:n], PB[6][0:1, 0:n], 1.0 / D),
            reads=["PB6"], writes=["ROWA"])
        b.v(lambda e: e.tensor_scalar_mul(ROWB[0:1, 0:n], PB[6][32:33, 0:n], 1.0 / D),
            reads=["PB6"], writes=["ROWB"])

    def emit_ln(t0, own):
        for c in range(NDC):
            b.dl(lambda e, c=c: e.dma_start(out=XS(c), in_=xb[c * 128:(c + 1) * 128, t0:t0 + SC]),
                 writes=["XS"])
        for c in range(NDC):
            b.dg(lambda e, c=c: e.dma_start(out=XSB(c), in_=xb[c * 128:(c + 1) * 128, t0:t0 + SC]),
                 writes=["XSB"])
        b.s(lambda e: e.activation(AR4[:, NDC * SC:2 * NDC * SC], AR3[:, 0:NDC * SC],
                                   AF.Square, bias=0.0, scale=1.0),
            reads=["XS"], writes=["XSQ"])
        for c in range(NDC):
            b.t(lambda e, c=c: e.matmul(PB[6][0:1, 0:SC], ONESC[:], XSB(c),
                                        start=(c == 0), stop=(c == NDC - 1)),
                reads=["XSB", "ONESC"], writes=["PB6"])
        for c in range(NDC):
            b.t(lambda e, c=c: e.matmul(PB[6][32:33, 0:SC], ONESC[:], XSQ(c),
                                        start=(c == 0), stop=(c == NDC - 1)),
                reads=["XSQ", "ONESC"], writes=["PB6"])
        ln_finalize(SC)
        tl = t0 - PRE
        b.t(lambda e: e.matmul(PB[7][:, 0:SC], ONESR[0:1, :], ROWA[0:1, 0:SC], start=True, stop=True),
            reads=["ROWA", "ONESR"], writes=["PB7"])
        b.t(lambda e: e.matmul(PB[6][:, 0:SC], ONESR[64:65, :], ROWB[0:1, 0:SC], start=True, stop=True),
            reads=["ROWB", "ONESR", "PB6"], writes=["PB6"])
        b.v(lambda e: e.tensor_copy(RREP[:, 0:SC], PB[7][:, 0:SC]), reads=["PB7"], writes=["RREP0"])
        for c in range(NDC):
            b.v(lambda e, c=c: e.tensor_sub(XS(c), XS(c), RREP[:, 0:SC]),
                reads=["RREP0"], writes=["XS"])
        b.s(lambda e: e.activation(RREP[:, SC:2 * SC], RREP[:, 0:SC], AF.Square, bias=0.0, scale=1.0),
            reads=["RREP0"], writes=["RREP1"])
        b.v(lambda e: e.tensor_sub(RREP[:, 0:SC], PB[6][:, 0:SC], RREP[:, SC:2 * SC]),
            reads=["PB6", "RREP1", "RREP0"], writes=["RREP0"])
        if own:
            b.s(lambda e: e.activation(RREP[:, SC:2 * SC], RREP[:, 0:SC], AF.Sqrt,
                                       bias=1e-11, scale=1.000001),
                reads=["RREP0", "RREP1", "CONSTS"], writes=["RREP1"])
            b.v(lambda e: e.reciprocal(RREP[:, SC:2 * SC], RREP[:, SC:2 * SC]),
                reads=["RREP1"], writes=["RREP1"])
            for c in range(NDC):
                b.v(lambda e, c=c: e.tensor_mul(ST[:, c * TOK + tl:c * TOK + tl + SC],
                                                XS(c), RREP[:, SC:2 * SC]),
                    reads=["XS", "RREP1"], writes=["ST"])
        b.s(lambda e: e.activation(RREP[:, 0:SC], RREP[:, 0:SC], AF.Sqrt, bias=1e-5, scale=1.0),
            reads=["RREP0", "RREP1", "CONSTS"], writes=["RREP0"])
        b.v(lambda e: e.reciprocal(RREP[:, 0:SC], RREP[:, 0:SC]), reads=["RREP0"], writes=["RREP0"])
        for c in range(NDC):
            b.v(lambda e, c=c: e.scalar_tensor_tensor(
                HSC[:, c * SC:(c + 1) * SC], XS(c), LN1[:, c:c + 1], RREP[:, 0:SC],
                ALU.mult, ALU.mult),
                reads=["XS", "LN1", "RREP0"], writes=["HSC"])

    def emit_proj(own):
        WSTALL = ("WCH0", "WCH1", "A2")
        def wload(wname):
            for c in range(NDC):
                b.dg(lambda e, wn=wname, c=c: e.dma_start(
                    out=WST[:, c * D:(c + 1) * D],
                    in_=din[wn][c * 128:(c + 1) * 128, :]),
                    writes=WSTALL)
        if own:
            wload("wq")
            for oc in range(NDC):
                pn = f"PB{oc % 2}"
                for c in range(NDC):
                    b.t(lambda e, oc=oc, c=c: e.matmul(
                        PB[oc % 2][:, 0:SC],
                        WST[:, c * D + oc * 128: c * D + oc * 128 + 128],
                        HSC[:, c * SC:(c + 1) * SC],
                        start=(c == 0), stop=(c == NDC - 1)),
                        reads=["HSC"] + list(WSTALL), writes=[pn])
                b.v(lambda e, oc=oc: e.tensor_copy(QF(oc), PB[oc % 2][:, 0:SC]),
                    reads=[pn], writes=["QF"])
        wload("wk")
        if own:
            for oc in range(NDC):
                pn = f"PB{oc % 2}"
                for c in range(NDC):
                    b.t(lambda e, oc=oc, c=c: e.matmul(
                        PB[oc % 2][:, 0:SC],
                        WST[:, c * D + oc * 128: c * D + oc * 128 + 128],
                        HSC[:, c * SC:(c + 1) * SC],
                        start=(c == 0), stop=(c == NDC - 1)),
                        reads=["HSC"] + list(WSTALL), writes=[pn])
                b.v(lambda e, oc=oc: e.tensor_copy(KFt(oc), PB[oc % 2][:, 0:SC]),
                    reads=[pn], writes=["KF"])
        for sub in range(NSUB):
            for half in range(2):
                pn = f"PB{half}"
                for c in range(NDC):
                    b.t(lambda e, sub=sub, c=c, half=half: e.matmul(
                        PB[half][:, 0:512],
                        HSC[:, c * SC + sub * 128: c * SC + sub * 128 + 128],
                        WST[:, c * D + half * 512: c * D + half * 512 + 512],
                        start=(c == 0), stop=(c == NDC - 1)),
                        reads=["HSC"] + list(WSTALL), writes=[pn])
                b.v(lambda e, sub=sub, half=half: e.tensor_copy(
                    KTOK(sub)[:, half * 512:(half + 1) * 512], PB[half][:, 0:512]),
                    reads=[pn], writes=["KTOK"])
        wload("wv")
        for sub in range(NSUB):
            for half in range(2):
                pn = f"PB{half}"
                for c in range(NDC):
                    b.t(lambda e, sub=sub, c=c, half=half: e.matmul(
                        PB[half][:, 0:512],
                        HSC[:, c * SC + sub * 128: c * SC + sub * 128 + 128],
                        WST[:, c * D + half * 512: c * D + half * 512 + 512],
                        start=(c == 0), stop=(c == NDC - 1)),
                        reads=["HSC"] + list(WSTALL), writes=[pn])
                b.v(lambda e, sub=sub, half=half: e.tensor_copy(
                    VTOK(sub)[:, half * 512:(half + 1) * 512], PB[half][:, 0:512]),
                    reads=[pn], writes=["VTOK"])
        for sub in range(NSUB):
            for c in range(NDC):
                b.t(lambda e, sub=sub, c=c: e.matmul(
                    PB[2][:, 0:2 * H],
                    HSC[:, c * SC + sub * 128: c * SC + sub * 128 + 128],
                    WIF[:, c * 2 * H:(c + 1) * 2 * H],
                    start=(c == 0), stop=(c == NDC - 1)),
                    reads=["HSC", "WIF"], writes=["PB2"])
            b.v(lambda e, sub=sub: e.tensor_add(GZR[:, sub * 2 * H:(sub + 1) * 2 * H],
                                                PB[2][:, 0:2 * H], BIFR[:]),
                reads=["PB2", "BIFR"], writes=["GZR"])
            b.v(lambda e, sub=sub: e.tensor_scalar(
                out=ZETA[:, sub * H:(sub + 1) * H],
                in0=GZR[:, sub * 2 * H: sub * 2 * H + H],
                scalar1=-10.0, scalar2=10.0, op0=ALU.max, op1=ALU.min),
                reads=["GZR"], writes=["ZETA"])
            b.s(lambda e, sub=sub: e.activation(LFT[:, sub * H:(sub + 1) * H],
                                                GZR[:, sub * 2 * H + H: sub * 2 * H + 2 * H],
                                                AF.Sigmoid, bias=0.0, scale=1.0),
                reads=["GZR"], writes=["LFT"])
            b.v(lambda e, sub=sub: e.tensor_scalar_max(LFT[:, sub * H:(sub + 1) * H],
                                                       LFT[:, sub * H:(sub + 1) * H], 1e-30),
                reads=["LFT"], writes=["LFT"])
            b.s(lambda e, sub=sub: e.activation(LFT[:, sub * H:(sub + 1) * H],
                                                LFT[:, sub * H:(sub + 1) * H],
                                                AF.Ln, bias=0.0, scale=1.0),
                reads=["LFT"], writes=["LFT"])
            b.v(lambda e, sub=sub: e.tensor_scalar_mul(NLFT[:, sub * H:(sub + 1) * H],
                                                       LFT[:, sub * H:(sub + 1) * H], -1.0),
                reads=["LFT"], writes=["NLFT"])
        if own:
            wload("wo")

    def grpoff(h):
        return (h // 8) * 16

    def emit_chunk(own, sub, tglob=0):
        so = sub * H
        toff = sub * 128
        b.v(lambda e: e.tensor_copy(SCCB[0:64, :], SCC[:]), reads=["SCC"], writes=["SCCB"])
        b.v(lambda e: e.tensor_copy(SCNB[0:64, :], SCN[:]), reads=["SCN"], writes=["SCNB"])
        b.dg(lambda e: e.dma_start(out=SCCB[64:128, :], in_=SCC[:]), reads=["SCC"], writes=["SCCBH"])
        b.dg(lambda e: e.dma_start(out=SCNB[64:128, :], in_=SCN[:]), reads=["SCN"], writes=["SCNBH"])
        b.t(lambda e: e.matmul(PB[6][0:128, 0:H], TRI[:], LFT[:, so:so + H],
                               start=True, stop=True),
            reads=["TRI", "LFT"], writes=["PB6"])
        b.t(lambda e: e.matmul(PB[6][0:1, 16:16 + H], ONESC32[:], LFT[:, so:so + H],
                               start=True, stop=True),
            reads=["LFT", "ONESC32"], writes=["PB6"])
        b.v(lambda e: e.tensor_copy(CUMT[:], PB[6][0:128, 0:H]), reads=["PB6"], writes=["CUMT"])
        b.v(lambda e: e.tensor_copy(CUML[:], PB[6][0:1, 16:16 + H]), reads=["PB6"], writes=["CUML"])
        b.t(lambda e: e.matmul(PB[6][:, 16:16 + H], ONESR[0:1, :], CUML[:], start=True, stop=True),
            reads=["CUML", "ONESR"], writes=["PB6"])
        b.s(lambda e: e.activation(LAMR[:], PB[6][:, 16:16 + H], AF.Exp, bias=0.0, scale=1.0),
            reads=["PB6"], writes=["LAMR"])
        b.v(lambda e: e.tensor_sub(WTILD[:], ZETA[:, so:so + H], CUMT[:]),
            reads=["ZETA", "CUMT"], writes=["WTILDp"])
        b.v(lambda e: e.tensor_add(WTILD[:], WTILD[:], PB[6][:, 16:16 + H]),
            reads=["WTILDp", "PB6"], writes=["WTILDp"])
        b.s(lambda e: e.activation(WTILD[:], WTILD[:], AF.Exp, bias=0.0, scale=1.0),
            reads=["WTILDp"], writes=["WTILD"])
        b.v(lambda e: e.tensor_copy(WTB[:], WTILD[:]), reads=["WTILD"], writes=["WTB"])
        for h in range(H):
            b.v(lambda e, h=h, sub=sub: e.tensor_single_scalar(
                VWT[:, h * 64:(h + 1) * 64], VTOK(sub)[:, h * 64:h * 64 + 64],
                WTILD[:, h:h + 1], ALU.mult),
                reads=["VTOK", "WTILD", "PB7"], writes=[f"VWT{h}"])
        if own:
            b.s(lambda e: e.activation(D0T[:], CUMT[:], AF.Exp, bias=0.0, scale=1.0),
                reads=["CUMT"], writes=["D0T"])
            b.t(lambda e: e.matmul(PB[6][0:H, 128:256], LFT[:, so:so + H], TRI[:],
                                   start=True, stop=True),
                reads=["LFT", "TRI"], writes=["PB6"])
            b.v(lambda e: e.tensor_copy(CFB[:], PB[6][0:H, 128:256]), reads=["PB6"], writes=["CFB"])
            b.dl(lambda e: e.dma_start(out=CUMROWS, in_=CFB[:]),
                 reads=["CFB"], writes=["CUMROWS"])
            b.t(lambda e: e.matmul(PB[6][0:H, 128:256], ZETA[:, so:so + H], ID32[:],
                                   start=True, stop=False),
                reads=["ZETA", "ID32", "CFB"], writes=["PB6"])
            b.t(lambda e: e.matmul(PB[6][0:H, 128:256], NLFT[:, so:so + H], TRI[:],
                                   start=False, stop=True),
                reads=["NLFT", "TRI"], writes=["PB6"])
            b.v(lambda e: e.tensor_copy(CFB[:], PB[6][0:H, 128:256]), reads=["PB6"], writes=["CFB"])
            b.dl(lambda e: e.dma_start(out=BEXPROWS, in_=CFB[:]),
                 reads=["CFB"], writes=["BEXPROWS"])
            for grp in range(2):
                hs = list(range(grp * 8, grp * 8 + 8))
                for h in hs:
                    c64 = (64 * h) % 128
                    fc = (64 * h) // 128
                    sb_bank = PB[(h % 8) // 4]
                    sb_off = (h % 4) * 128
                    b.t(lambda e, h=h, c64=c64, fc=fc, sb_bank=sb_bank, sb_off=sb_off: e.matmul(
                        sb_bank[:, sb_off:sb_off + 128],
                        KFt(fc)[c64:c64 + 64, toff:toff + 128],
                        QF(fc)[c64:c64 + 64, toff:toff + 128],
                        start=True, stop=True),
                        reads=["KF", "QF", f"ABFr{h % 8}"], writes=[f"PB{(h % 8) // 4}"])
                for h in hs:
                    eb_bank = PB[2 + (h % 8) // 4]
                    eb_off = (h % 4) * 128
                    b.t(lambda e, h=h, eb_bank=eb_bank, eb_off=eb_off: e.matmul(
                        eb_bank[:, eb_off:eb_off + 128],
                        ONESR[0:1, :], CUMROWS[0:1, h * 128:(h + 1) * 128],
                        start=True, stop=False),
                        reads=["CUMROWS", f"APREr{h % 8}"], writes=[f"PB{2 + (h % 8) // 4}"])
                    b.t(lambda e, h=h, eb_bank=eb_bank, eb_off=eb_off: e.matmul(
                        eb_bank[:, eb_off:eb_off + 128],
                        BEXPROWS[0:1, h * 128:(h + 1) * 128], ONESR[64:65, :],
                        start=False, stop=True),
                        reads=["BEXPROWS"], writes=[f"PB{2 + (h % 8) // 4}"])
                for h in hs:
                    eb_bank = PB[2 + (h % 8) // 4]
                    eb_off = (h % 4) * 128
                    b.v(lambda e, h=h, eb_bank=eb_bank, eb_off=eb_off: e.scalar_tensor_tensor(
                        APRE[:, (h % 8) * 128:(h % 8) * 128 + 128],
                        eb_bank[:, eb_off:eb_off + 128], 11.0, MASKE[:], ALU.min, ALU.add),
                        reads=[f"PB{2 + (h % 8) // 4}", "MASKE"], writes=[f"APRE{h % 8}"])
                for h in hs:
                    b.s(lambda e, h=h: e.activation(
                        APRE[:, (h % 8) * 128:(h % 8) * 128 + 128],
                        APRE[:, (h % 8) * 128:(h % 8) * 128 + 128],
                        AF.Exp, bias=0.0, scale=1.0),
                        reads=[f"APRE{h % 8}"], writes=[f"APRE{h % 8}", f"APREr{h % 8}"])
                for h in hs:
                    sb_bank = PB[(h % 8) // 4]
                    sb_off = (h % 4) * 128
                    b.v(lambda e, h=h, sb_bank=sb_bank, sb_off=sb_off: e.tensor_mul(
                        ABF[:, (h % 8) * 128:(h % 8) * 128 + 128],
                        APRE[:, (h % 8) * 128:(h % 8) * 128 + 128],
                        sb_bank[:, sb_off:sb_off + 128]),
                        reads=[f"APRE{h % 8}", f"PB{(h % 8) // 4}"], writes=[f"ABF{h % 8}", f"ABFr{h % 8}"])
                for h in hs:
                    b.t(lambda e, h=h, sub=sub: e.matmul(
                        PB[4][:, (h % 8) * 64:(h % 8) * 64 + 64],
                        ABF[:, (h % 8) * 128:(h % 8) * 128 + 128],
                        VTOK(sub)[:, h * 64:h * 64 + 64],
                        start=True, stop=True),
                        reads=[f"ABF{h % 8}", "VTOK", f"YTr{h}"], writes=["PB4"])
                    b.t(lambda e, h=h, grp=grp: e.matmul(
                        PB[6][0:128, 256 + grpoff(h) + (h % 8):257 + grpoff(h) + (h % 8)],
                        ABF[:, (h % 8) * 128:(h % 8) * 128 + 128], ONESC[:],
                        start=True, stop=True),
                        reads=[f"ABF{h % 8}", "ONESC", f"DNTg{grp}"], writes=["PB6"])
                for h in hs:
                    c64 = (64 * h) % 128
                    fc = (64 * h) // 128
                    b.t(lambda e, h=h, c64=c64, fc=fc: e.matmul(
                        PB[5][:, (h % 8) * 64:(h % 8) * 64 + 64],
                        QF(fc)[c64:c64 + 64, toff:toff + 128],
                        SCCB[c64:c64 + 64, h * 64:(h + 1) * 64],
                        start=True, stop=True),
                        reads=["QF", "SCCB", "SCCBH", f"YTr{h}"], writes=["PB5"])
                    b.t(lambda e, h=h, c64=c64, fc=fc, grp=grp: e.matmul(
                        PB[6][0:128, 288 + grpoff(h) + (h % 8):289 + grpoff(h) + (h % 8)],
                        QF(fc)[c64:c64 + 64, toff:toff + 128],
                        SCNB[c64:c64 + 64, h:h + 1],
                        start=True, stop=True),
                        reads=["QF", "SCNB", "SCNBH", f"DNTg{grp}"], writes=["PB6"])
                b.v(lambda e, grp=grp: e.tensor_mul(
                    DNT[:, grp * 8:grp * 8 + 8],
                    PB[6][0:128, 288 + grp * 16:288 + grp * 16 + 8],
                    D0T[:, grp * 8:grp * 8 + 8]),
                    reads=["PB6", "D0T"], writes=[f"DNTg{grp}"])
                b.v(lambda e, grp=grp: e.tensor_add(
                    DNT[:, grp * 8:grp * 8 + 8], DNT[:, grp * 8:grp * 8 + 8],
                    PB[6][0:128, 256 + grp * 16:256 + grp * 16 + 8]),
                    reads=["PB6", f"DNTg{grp}"], writes=[f"DNTg{grp}"])
                b.v(lambda e, grp=grp: e.tensor_scalar(
                    out=DNT[:, 16 + grp * 8:16 + grp * 8 + 8],
                    in0=DNT[:, grp * 8:grp * 8 + 8],
                    scalar1=-1.0, scalar2=1.0, op0=ALU.mult, op1=ALU.max),
                    reads=[f"DNTg{grp}"], writes=[f"DNTn{grp}"])
                b.v(lambda e, grp=grp: e.tensor_max(
                    DNT[:, grp * 8:grp * 8 + 8], DNT[:, grp * 8:grp * 8 + 8],
                    DNT[:, 16 + grp * 8:16 + grp * 8 + 8]),
                    reads=[f"DNTn{grp}", f"DNTg{grp}"], writes=[f"DNTg{grp}"])
                b.v(lambda e, grp=grp: e.reciprocal(
                    DNT[:, 16 + grp * 8:16 + grp * 8 + 8], DNT[:, grp * 8:grp * 8 + 8]),
                    reads=[f"DNTg{grp}"], writes=[f"DNTr{grp}"])
                for h in hs:
                    b.v(lambda e, h=h: e.tensor_single_scalar(
                        YT[:, h * 64:(h + 1) * 64],
                        PB[5][:, (h % 8) * 64:(h % 8) * 64 + 64], D0T[:, h:h + 1], ALU.mult),
                        reads=["PB5", "D0T"], writes=[f"YT{h}"])
                    b.v(lambda e, h=h: e.scalar_tensor_tensor(
                        YT[:, h * 64:(h + 1) * 64], PB[4][:, (h % 8) * 64:(h % 8) * 64 + 64],
                        1.0, YT[:, h * 64:(h + 1) * 64], ALU.mult, ALU.add),
                        reads=["PB4", f"YT{h}"], writes=[f"YT{h}"])
                    b.v(lambda e, h=h, grp=grp: e.tensor_single_scalar(
                        YT[:, h * 64:(h + 1) * 64], YT[:, h * 64:(h + 1) * 64],
                        DNT[:, 16 + h:16 + h + 1], ALU.mult),
                        reads=[f"YT{h}", f"DNTr{grp}"], writes=[f"YT{h}", f"YTr{h}"])
            for c in range(NDC):
                b.t(lambda e, c=c: e.transpose(PB[6][:, 320:448], YT[:, c * 128:(c + 1) * 128],
                                               ID32[:]),
                    reads=[f"YT{2 * c}", f"YT{2 * c + 1}", "ID32", "YFp"], writes=["PB6"])
                b.v(lambda e, c=c: e.tensor_copy(YF[:, c * 128:(c + 1) * 128], PB[6][:, 320:448]),
                    reads=["PB6"], writes=["YFp"])
            for oc in range(NDC):
                for c in range(NDC):
                    b.t(lambda e, oc=oc, c=c: e.matmul(
                        PB[0][:, 0:128],
                        WST[:, c * D + oc * 128: c * D + oc * 128 + 128],
                        YF[:, c * 128:(c + 1) * 128],
                        start=(c == 0), stop=(c == NDC - 1)),
                        reads=["YFp", "WCH0", "WCH1", "A2"], writes=["PB0"])
                b.v(lambda e, oc=oc, toffv=sub * 128: e.tensor_add(
                    ST[:, oc * TOK + tglob + toffv: oc * TOK + tglob + toffv + 128],
                    ST[:, oc * TOK + tglob + toffv: oc * TOK + tglob + toffv + 128],
                    PB[0][:, 0:128]),
                    reads=["PB0"], writes=["ST"])
        for grp in range(2):
            hs = list(range(grp * 8, grp * 8 + 8))
            for h in hs:
                b.t(lambda e, h=h, sub=sub: e.matmul(
                    PB[7][0:64, (h % 8) * 64:(h % 8) * 64 + 64],
                    KTOK(sub)[:, h * 64:h * 64 + 64], VWT[:, h * 64:(h + 1) * 64],
                    start=True, stop=True),
                    reads=["KTOK", f"VWT{h}", f"SCCd{h}"], writes=["PB7"])
                b.t(lambda e, h=h, sub=sub, grp=grp: e.matmul(
                    PB[6][0:64, 448 + grpoff(h) + (h % 8):449 + grpoff(h) + (h % 8)],
                    KTOK(sub)[:, h * 64:h * 64 + 64], WTB[:, h:h + 1],
                    start=True, stop=True),
                    reads=["KTOK", "WTB", f"SCNd{h}"], writes=["PB6"])
            for h in hs:
                b.v(lambda e, h=h: e.scalar_tensor_tensor(
                    SCC[:, h * 64:(h + 1) * 64], SCC[:, h * 64:(h + 1) * 64],
                    LAMR[0:64, h:h + 1], PB[7][0:64, (h % 8) * 64:(h % 8) * 64 + 64],
                    ALU.mult, ALU.add),
                    reads=["PB7", "LAMR", "SCCB", "SCCBH"], writes=[f"SCCd{h}"])
                b.v(lambda e, h=h, grp=grp: e.scalar_tensor_tensor(
                    SCN[:, h:h + 1], SCN[:, h:h + 1], LAMR[0:64, h:h + 1],
                    PB[6][0:64, 448 + grpoff(h) + (h % 8):449 + grpoff(h) + (h % 8)],
                    ALU.mult, ALU.add),
                    reads=["PB6", "LAMR", "SCNB", "SCNBH"], writes=[f"SCNd{h}"])
        b.v(lambda e: e.memset(YSCR[0:1, 0:1], 0.0),
            reads=[f"SCCd{h}" for h in range(H)] + [f"SCNd{h}" for h in range(H)],
            writes=["SCC", "SCN"])

    if PRE:
        for q in range(3):
            if q >= 1:
                base = (q - 1) * H * (DD + 1)
                b.v(lambda e, base=base: e.tensor_copy(
                    SNAP[:, base: base + H * DD], SCC[:]),
                    reads=["SCC"], writes=["SNAP"])
                b.v(lambda e, base=base: e.tensor_copy(
                    SNAP[:, base + H * DD: base + H * (DD + 1)], SCN[:]),
                    reads=["SCN"], writes=["SNAP"])
            for sci in range(QUART // SC):
                emit_ln(q * QUART + sci * SC, own=False)
                emit_proj(own=False)
                for sub in range(NSUB):
                    emit_chunk(own=False, sub=sub)
        b.v(lambda e: e.tensor_single_scalar(SCC[:], SCC[:], OH[0:64, 3:4], ALU.mult),
            reads=["OH"], writes=["SCC"])
        b.v(lambda e: e.tensor_single_scalar(SCN[:], SCN[:], OH[0:64, 3:4], ALU.mult),
            reads=["OH"], writes=["SCN"])
        for q in (1, 2):
            base = (q - 1) * H * (DD + 1)
            b.v(lambda e, base=base, q=q: e.scalar_tensor_tensor(
                SCC[:], SNAP[:, base:base + H * DD], OH[0:64, q:q + 1], SCC[:],
                ALU.mult, ALU.add),
                reads=["SNAP", "OH"], writes=["SCC"])
            b.v(lambda e, base=base, q=q: e.scalar_tensor_tensor(
                SCN[:], SNAP[:, base + H * DD:base + H * (DD + 1)], OH[0:64, q:q + 1], SCN[:],
                ALU.mult, ALU.add),
                reads=["SNAP", "OH"], writes=["SCN"])

    for s in range(NSC):
        emit_ln(PRE + s * SC, own=True)
        emit_proj(own=True)
        for sub in range(NSUB):
            emit_chunk(own=True, sub=sub, tglob=s * SC)

    b.v(lambda e: e.memset(YSCR[0:1, 1:2], 0.0),
        reads=["QF", "KF", "KTOK", "VTOK"], writes=["SWT"])

    for s in range(NSC):
        tl = s * SC
        for c in range(NDC):
            b.s(lambda e, c=c, tl=tl: e.activation(
                XSQ(c), ST[:, c * TOK + tl:c * TOK + tl + SC],
                AF.Square, bias=0.0, scale=1.0),
                reads=["ST"], writes=["XSQ"])
        for c in range(NDC):
            b.t(lambda e, c=c: e.matmul(PB[6][32:33, 0:SC], ONESC[:], XSQ(c),
                                        start=(c == 0), stop=(c == NDC - 1)),
                reads=["XSQ", "ONESC"], writes=["PB6"])
        b.v(lambda e: e.tensor_scalar_mul(ROWB[0:1, 0:SC], PB[6][32:33, 0:SC], 1.0 / D),
            reads=["PB6"], writes=["ROWB"])
        b.s(lambda e: e.activation(ROWB[0:1, 0:SC], ROWB[0:1, 0:SC], AF.Sqrt, bias=1e-6, scale=1.0),
            reads=["ROWB", "CONSTS"], writes=["ROWB"])
        b.v(lambda e: e.reciprocal(ROWB[0:1, 0:SC], ROWB[0:1, 0:SC]), reads=["ROWB"], writes=["ROWB"])
        b.t(lambda e: e.matmul(PB[7][:, 0:SC], ONESR[64:65, :], ROWB[0:1, 0:SC], start=True, stop=True),
            reads=["ROWB", "ONESR"], writes=["PB7"])
        b.v(lambda e: e.tensor_copy(RREP[:, 0:SC], PB[7][:, 0:SC]), reads=["PB7"], writes=["RREP0"])
        for c in range(NDC):
            b.v(lambda e, c=c, tl=tl: e.scalar_tensor_tensor(
                HSC[:, c * SC:(c + 1) * SC],
                ST[:, c * TOK + tl:c * TOK + tl + SC], LN2[:, c:c + 1], RREP[:, 0:SC],
                ALU.mult, ALU.mult),
                reads=["ST", "LN2", "RREP0"], writes=["HSC"])
        for g in range(NGC):
            for c in range(NDC):
                b.dg(lambda e, g=g, c=c: e.dma_start(
                    out=WCH(0)[:, c * 128:(c + 1) * 128],
                    in_=din["wg"][g][c * 128:(c + 1) * 128, :]),
                    writes=["WCH0"])
            for c in range(NDC):
                b.dg(lambda e, g=g, c=c: e.dma_start(
                    out=WCH(1)[:, c * 128:(c + 1) * 128],
                    in_=din["wu"][g][c * 128:(c + 1) * 128, :]),
                    writes=["WCH1"])
            for c in range(NDC):
                b.t(lambda e, g=g, c=c: e.matmul(
                    PB[0][:, 0:SC], WCH(0)[:, c * 128:(c + 1) * 128],
                    HSC[:, c * SC:(c + 1) * SC],
                    start=(c == 0), stop=(c == NDC - 1)),
                    reads=["WCH0", "HSC"], writes=["PB0"])
            for c in range(NDC):
                b.t(lambda e, g=g, c=c: e.matmul(
                    PB[1][:, 0:SC], WCH(1)[:, c * 128:(c + 1) * 128],
                    HSC[:, c * SC:(c + 1) * SC],
                    start=(c == 0), stop=(c == NDC - 1)),
                    reads=["WCH1", "HSC"], writes=["PB1"])
            b.s(lambda e: e.activation(SIL[:], PB[0][:, 0:SC], AF.Silu, bias=0.0, scale=1.0),
                reads=["PB0"], writes=["SIL"])
            b.v(lambda e, g=g: e.tensor_mul(SWTc(g), SIL[:], PB[1][:, 0:SC]),
                reads=["SIL", "PB1"], writes=["SWT"])
        for oc in range(NDC):
            for g in range(NGC):
                slot = g % 2
                b.dg(lambda e, g=g, oc=oc, slot=slot: e.dma_start(
                    out=WCH(slot)[:, 0:128],
                    in_=din["wd"][oc][g * 128:(g + 1) * 128, :]),
                    writes=[f"WCH{slot}"])
                b.t(lambda e, g=g, slot=slot: e.matmul(
                    PB[3][:, 0:SC], WCH(slot)[:, 0:128], SWTc(g),
                    start=(g == 0), stop=(g == NGC - 1)),
                    reads=[f"WCH{slot}", "SWT"], writes=["PB3"])
            b.v(lambda e, oc=oc, tl=tl: e.tensor_add(
                ST[:, oc * TOK + tl:oc * TOK + tl + SC],
                ST[:, oc * TOK + tl:oc * TOK + tl + SC], PB[3][:, 0:SC]),
                reads=["PB3"], writes=["ST"])
        for c in range(NDC):
            b.dl(lambda e, c=c, tl=tl: e.dma_start(
                out=XS(c), in_=xb[c * 128:(c + 1) * 128, PRE + tl:PRE + tl + SC]),
                writes=["XS"])
        for c in range(NDC):
            b.v(lambda e, c=c, tl=tl: e.tensor_add(
                ST[:, c * TOK + tl:c * TOK + tl + SC],
                ST[:, c * TOK + tl:c * TOK + tl + SC], XS(c)),
                reads=["XS"], writes=["ST"])

    b.v(lambda e: e.memset(YSCR[0:1, 2:3], 0.0), reads=["SWT"], writes=["HQ"])

    def chain(levels, rmain, rname, rtmp, tmpname, keeps):
        for c in levels:
            b.v(lambda e, c=c: e.tensor_scalar(out=rtmp, in0=rmain, scalar1=float(c),
                                               scalar2=float(c), op0=ALU.is_ge, op1=ALU.mult),
                reads=[rname], writes=[tmpname])
            b.v(lambda e: e.tensor_sub(rmain, rmain, rtmp),
                reads=[tmpname, rname], writes=[rname])
            if c in keeps:
                dst, dname = keeps[c]
                b.v(lambda e, dst=dst: e.tensor_copy(dst, rmain), reads=[rname], writes=[dname])

    for ft in range(NFT):
        tl = ft * FT
        for c in range(NDC):
            b.s(lambda e, c=c, tl=tl: e.activation(
                XSQ(c)[:, 0:FT], ST[:, c * TOK + tl:c * TOK + tl + FT],
                AF.Square, bias=0.0, scale=1.0), reads=["ST"], writes=["XSQ"])
            b.v(lambda e, c=c, tl=tl: e.tensor_copy(
                XSB(c)[:, 0:FT], ST[:, c * TOK + tl:c * TOK + tl + FT]),
                reads=["ST"], writes=["XSB"])
        for c in range(NDC):
            b.t(lambda e, c=c: e.matmul(PB[6][0:1, 0:FT], ONESC[:], XSB(c)[:, 0:FT],
                                        start=(c == 0), stop=(c == NDC - 1)),
                reads=["XSB", "ONESC"], writes=["PB6"])
        for c in range(NDC):
            b.t(lambda e, c=c: e.matmul(PB[6][32:33, 0:FT], ONESC[:], XSQ(c)[:, 0:FT],
                                        start=(c == 0), stop=(c == NDC - 1)),
                reads=["XSQ", "ONESC"], writes=["PB6"])
        b.v(lambda e: e.tensor_scalar_mul(ROWA[0:1, 0:FT], PB[6][0:1, 0:FT], 1.0 / D),
            reads=["PB6"], writes=["ROWA"])
        b.v(lambda e: e.tensor_scalar_mul(ROWB[0:1, 0:FT], PB[6][32:33, 0:FT], 1.0 / D),
            reads=["PB6"], writes=["ROWB"])
        b.t(lambda e: e.matmul(PB[7][:, 0:FT], ONESR[0:1, :], ROWA[0:1, 0:FT], start=True, stop=True),
            reads=["ROWA", "ONESR"], writes=["PB7"])
        b.t(lambda e: e.matmul(PB[6][:, 0:FT], ONESR[64:65, :], ROWB[0:1, 0:FT], start=True, stop=True),
            reads=["ROWB", "ONESR", "PB6"], writes=["PB6"])
        b.v(lambda e: e.tensor_copy(RREP[:, 0:FT], PB[7][:, 0:FT]), reads=["PB7"], writes=["RREP0"])
        for c in range(NDC):
            b.v(lambda e, c=c, tl=tl: e.tensor_sub(
                XS(c)[:, 0:FT], ST[:, c * TOK + tl:c * TOK + tl + FT], RREP[:, 0:FT]),
                reads=["ST", "RREP0"], writes=["XS"])
        b.s(lambda e: e.activation(RREP[:, SC:SC + FT], RREP[:, 0:FT], AF.Square, bias=0.0, scale=1.0),
            reads=["RREP0"], writes=["RREP1"])
        b.v(lambda e: e.tensor_sub(RREP[:, 0:FT], PB[6][:, 0:FT], RREP[:, SC:SC + FT]),
            reads=["PB6", "RREP1", "RREP0"], writes=["RREP0"])
        b.s(lambda e: e.activation(RREP[:, 0:FT], RREP[:, 0:FT], AF.Sqrt, bias=1e-5, scale=1.0),
            reads=["RREP0", "CONSTS"], writes=["RREP0"])
        b.v(lambda e: e.reciprocal(RREP[:, 0:FT], RREP[:, 0:FT]), reads=["RREP0"], writes=["RREP0"])
        for c in range(NDC):
            b.v(lambda e, c=c: e.tensor_mul(A2(c), XS(c)[:, 0:FT], RREP[:, 0:FT]),
                reads=["XS", "RREP0"], writes=["A2"])
        for fcid in range(NFC):
            slot = fcid % 2
            for c in range(NDC):
                b.dg(lambda e, fcid=fcid, slot=slot, c=c: e.dma_start(
                    out=WCH(slot)[:, c * 128:(c + 1) * 128],
                    in_=din["w1"][fcid][c * 128:(c + 1) * 128, :]),
                    writes=[f"WCH{slot}"])
            for c in range(NDC):
                b.t(lambda e, fcid=fcid, c=c, slot=slot: e.matmul(
                    PB[slot][:, 0:FT], WCH(slot)[:, c * 128:(c + 1) * 128], A2(c),
                    start=(c == 0), stop=(c == NDC - 1)),
                    reads=[f"WCH{slot}", "A2"], writes=[f"PB{slot}"])
            b.v(lambda e, fcid=fcid, slot=slot: e.tensor_scalar(
                out=U(0), in0=PB[slot][:, 0:FT], scalar1=256.0,
                scalar2=B1S[:, fcid:fcid + 1], op0=ALU.mult, op1=ALU.add),
                reads=[f"PB{slot}", "B1S"], writes=["Uu"])
            b.v(lambda e: e.tensor_scalar(out=U(1), in0=U(0), scalar1=0.5, scalar2=2176.5,
                                          op0=ALU.max, op1=ALU.min),
                reads=["Uu"], writes=["Uc"])
            b.v(lambda e: e.tensor_copy(U(2), U(1)), reads=["Uc"], writes=["Ur"])
            chain(CHAIN1, U(2), "Ur", U(3), "Utmp", {64: (U(4), "Ur6")})
            b.v(lambda e: e.tensor_sub(U(5), U(1), U(4)), reads=["Uc", "Ur6"], writes=["Uiarg", "FOUT"])
            b.v(lambda e: e.tensor_sub(U(4), U(4), U(2)), reads=["Ur6", "Ur"], writes=["Ufr64"])
            b.s(lambda e: e.activation(GAR, U(5), AF.Gelu, bias=-4.25, scale=1.0 / 256.0),
                reads=["Uiarg", "CONSTS", "SIL"], writes=["Uga"])
            b.s(lambda e: e.activation(GBR, U(5), AF.Gelu, bias=-4.0, scale=1.0 / 256.0),
                reads=["Uiarg", "CONSTS", "SIL"], writes=["Ugb"])
            b.v(lambda e: e.tensor_sub(GBR, GBR, GAR), reads=["Ugb", "Uga"], writes=["Ugb"])
            b.v(lambda e: e.tensor_mul(U(4), U(4), GBR), reads=["Ufr64", "Ugb"], writes=["Ufr64"])
            b.v(lambda e: e.tensor_sub(U(6), U(1), U(2)), reads=["Uc", "Ur"], writes=["Uxq"])
            b.v(lambda e: e.tensor_scalar_add(U(6), U(6), -1088.0), reads=["Uxq"], writes=["Uxq"])
            b.v(lambda e: e.tensor_scalar(out=U(7), in0=U(6), scalar1=0.0, scalar2=0.5,
                                          op0=ALU.max, op1=ALU.add),
                reads=["Uxq"], writes=["Uz"])
            b.v(lambda e: e.scalar_tensor_tensor(U(3), GAR, -256.0, U(7), ALU.mult, ALU.add),
                reads=["Uga", "Uz"], writes=["Uv"])
            b.v(lambda e: e.scalar_tensor_tensor(U(3), U(4), -4.0, U(3), ALU.mult, ALU.add),
                reads=["Ufr64", "Uv"], writes=["Uv"])
            b.v(lambda e: e.tensor_copy(U(1), U(3)), reads=["Uv"], writes=["Uvk"])
            chain(CHAIN3, U(3), "Uv", U(2), "Utmp", {})
            b.v(lambda e: e.tensor_sub(U(1), U(1), U(3)), reads=["Uvk", "Uv"], writes=["Uvk"])
            b.v(lambda e: e.tensor_sub(U(1), U(7), U(1)), reads=["Uz", "Uvk"], writes=["Um05"])
            b.v(lambda e: e.tensor_scalar(out=U(2), in0=U(1), scalar1=1.0 / 256.0,
                                          scalar2=-0.001953125, op0=ALU.mult, op1=ALU.add),
                reads=["Um05"], writes=["Uhq"])
            b.v(lambda e: e.tensor_scalar(out=U(3), in0=U(0), scalar1=1.0 / 256.0,
                                          scalar2=-4.251953125, op0=ALU.mult, op1=ALU.add),
                reads=["Uu"], writes=["Uha"])
            b.v(lambda e: e.tensor_single_scalar(U(4), U(0), 2113.0, ALU.is_ge),
                reads=["Uu"], writes=["Umhi"])
            b.v(lambda e: e.tensor_sub(U(3), U(3), U(2)), reads=["Uha", "Uhq"], writes=["Uha"])
            b.v(lambda e: e.tensor_mul(U(3), U(3), U(4)), reads=["Uha", "Umhi"], writes=["Uha"])
            b.v(lambda e: e.tensor_add(U(5), U(2), U(3)), reads=["Uhq", "Uha"], writes=["Usel"])
            b.v(lambda e: e.tensor_single_scalar(U(6), U(0), 64.0, ALU.is_ge),
                reads=["Uu"], writes=["Umlo"])
            b.v(lambda e, fcid=fcid: e.tensor_mul(HQc(fcid), U(5), U(6)),
                reads=["Usel", "Umlo"], writes=["HQ"])
        for oc in range(NDC):
            pslot = oc % 2
            for fcid in range(NFC):
                wslot = fcid % 2
                b.dg(lambda e, oc=oc, fcid=fcid, wslot=wslot: e.dma_start(
                    out=WCH(wslot)[:, 0:128],
                    in_=din["w2"][oc][fcid * 128:(fcid + 1) * 128, :]),
                    writes=[f"WCH{wslot}"])
                b.t(lambda e, fcid=fcid, wslot=wslot, pslot=pslot: e.matmul(
                    PB[3 + pslot][:, 0:FT], WCH(wslot)[:, 0:128], HQc(fcid),
                    start=(fcid == 0), stop=(fcid == NFC - 1)),
                    reads=[f"WCH{wslot}", "HQ"], writes=[f"PB{3 + pslot}"])
            b.v(lambda e, oc=oc, pslot=pslot: e.tensor_scalar(
                out=U(0), in0=PB[3 + pslot][:, 0:FT], scalar1=256.0,
                scalar2=B2S[:, oc:oc + 1], op0=ALU.mult, op1=ALU.add),
                reads=[f"PB{3 + pslot}", "B2S"], writes=["Uu"])
            b.v(lambda e: e.tensor_scalar(out=U(1), in0=U(0), scalar1=0.5, scalar2=4095.5,
                                          op0=ALU.max, op1=ALU.min),
                reads=["Uu"], writes=["Uc"])
            b.v(lambda e: e.tensor_copy(U(2), U(1)), reads=["Uc"], writes=["Ur"])
            chain(CHAIN4, U(2), "Ur", U(3), "Utmp", {})
            b.v(lambda e: e.tensor_sub(U(1), U(1), U(2)), reads=["Uc", "Ur"], writes=["Uq"])
            b.v(lambda e: e.tensor_scalar(out=U(1), in0=U(1), scalar1=1.0 / 256.0,
                                          scalar2=-8.0, op0=ALU.mult, op1=ALU.add),
                reads=["Uq"], writes=["Uq"])
            b.v(lambda e, oc=oc, tl=tl: e.tensor_add(
                FOUT[:], U(1), ST[:, oc * TOK + tl:oc * TOK + tl + FT]),
                reads=["Uq", "ST"], writes=["FOUT", "Usel"])
            b.dl(lambda e, oc=oc, tl=tl: e.dma_start(
                out=out_t[oc * 128:(oc + 1) * 128, tl:tl + FT], in_=FOUT[:]),
                reads=["FOUT"], writes=["OUTDMA"], force_inc=True)

    for name, shape in debug:
        if name == "s1":
            b.dl(lambda e: e.dma_start(out=dbg_t["s1"], in_=ST[:, 0:NDC * TOK]),
                 reads=["ST"], writes=["DBGDMA"])
        if name == "hq":
            b.dl(lambda e: e.dma_start(out=dbg_t["hq"], in_=AR5[:, 0:NFC * FT]),
                 reads=["HQ"], writes=["DBGDMA"])

    b.v(lambda e: e.memset(YSCR[0:1, 3:4], 0.0), reads=["OUTDMA", "DBGDMA"], writes=[])

    with ExitStack() as st2:
        sems = {}
        for ename in B.ENGS:
            sems[ename] = st2.enter_context(nc.semaphore("sem_" + ename))
        for ename in ("vector", "tensor", "scalar"):
            for i2 in range(2):
                sems[ename + str(i2)] = st2.enter_context(nc.semaphore(f"sem_{ename}{i2}"))
        block = st2.enter_context(nc.Block())
        b.emit_into(block, sems, limit=limit)
    st.close()
    return nc, b


def make_consts():
    tri = np.zeros((L, L), np.float32)
    for s_ in range(L):
        tri[s_, s_:] = 1.0
    maskE = np.where(tri > 0, 0.0, -1.0e30).astype(np.float32)
    ident = np.eye(128, dtype=np.float32)
    return tri, maskE, ident


def make_inmaps(inputs, TOK=2048, PRE=6144):
    x = np.asarray(inputs["L0_attn_x"], np.float32)
    Bb, Tt, _ = x.shape
    tri, maskE, ident = make_consts()
    wif = np.concatenate([np.asarray(inputs["L0_attn_wi"], np.float32),
                          np.asarray(inputs["L0_attn_wf"], np.float32)], axis=1)
    bif = np.concatenate([np.asarray(inputs["L0_attn_bi"], np.float32),
                          np.asarray(inputs["L0_attn_bf"], np.float32)])
    bif_rep = np.tile(bif[None, :], (128, 1)).astype(np.float32)
    wg = np.asarray(inputs["L0_attn_w_gate"], np.float32).reshape(D, NGC, 128).transpose(1, 0, 2).copy()
    wu = np.asarray(inputs["L0_attn_w_up"], np.float32).reshape(D, NGC, 128).transpose(1, 0, 2).copy()
    wd = np.asarray(inputs["L0_attn_w_down"], np.float32).reshape(IG, NDC, 128).transpose(1, 0, 2).copy()
    w1 = np.asarray(inputs["L0_ffn_w1"], np.float32).reshape(D, NFC, 128).transpose(1, 0, 2).copy()
    w2 = np.asarray(inputs["L0_ffn_w2"], np.float32).reshape(FF, NDC, 128).transpose(1, 0, 2).copy()
    shared = dict(
        tri=tri, maskE=maskE, ident32=ident,
        wq=np.asarray(inputs["L0_attn_wq"], np.float32),
        wk=np.asarray(inputs["L0_attn_wk"], np.float32),
        wv=np.asarray(inputs["L0_attn_wv"], np.float32),
        wo=np.asarray(inputs["L0_attn_wo"], np.float32),
        wif=wif, bif_rep=bif_rep,
        ln1=np.ascontiguousarray(np.asarray(inputs["L0_attn_ln1_w"], np.float32).reshape(NDC, 128).T),
        ln2=np.ascontiguousarray(np.asarray(inputs["L0_attn_ln2_w"], np.float32).reshape(NDC, 128).T),
        wg=wg, wu=wu, wd=wd, w1=w1,
        b1=np.ascontiguousarray(np.asarray(inputs["L0_ffn_b1"], np.float32).reshape(NFC, 128).T),
        w2=w2, b2=np.ascontiguousarray(np.asarray(inputs["L0_ffn_b2"], np.float32).reshape(NDC, 128).T),
    )
    n_quart = Tt // TOK
    in_maps = []
    for c in range(8):
        bb = c // n_quart if n_quart < 8 else 0
        bb = c // 4
        qt = c % 4
        xt = np.ascontiguousarray(x[bb].T)
        pre = xt[:, 0:PRE]
        own = xt[:, qt * TOK:(qt + 1) * TOK]
        xbm = np.concatenate([pre, own], axis=1)
        onehot = np.zeros((128, 4), np.float32)
        onehot[:, qt] = 1.0
        m = dict(shared)
        m["xb"] = np.ascontiguousarray(xbm)
        m["onehot"] = onehot
        in_maps.append(m)
    return in_maps


def assemble(results, TOK=2048):
    outs = []
    for bb in range(2):
        quarters = [results[bb * 4 + qt]["out"] for qt in range(4)]
        full = np.concatenate(quarters, axis=1)
        outs.append(full.T)
    return np.stack(outs, axis=0).astype(np.float32)


def kernel(**inputs):
    from concourse.bass_utils import run_bass_kernel_spmd
    nc, b = build_program(TOK=2048, PRE=6144)
    in_maps = make_inmaps(inputs, TOK=2048, PRE=6144)
    res = run_bass_kernel_spmd(nc, in_maps, core_ids=list(range(8)))
    return assemble([r for r in res.results], TOK=2048)


# version 02 (final)
# speedup vs baseline: 1.5679x; 1.5679x over previous
import numpy as np
import ml_dtypes
from contextlib import ExitStack

import concourse.bass as bass
import concourse.mybir as mybir

F32 = mybir.dt.float32
BF16 = mybir.dt.bfloat16
AF = mybir.ActivationFunctionType
ALU = mybir.AluOpType

D = 1024; H = 16; DD = 64; FF = 4096; IG = 2816
L = 128; SC = 512
NDC = D // 128; NGC = IG // 128; NFC = FF // 128
NSUB = SC // L
CHAIN1 = [2048, 1024, 512, 256, 128, 64, 32, 16, 8, 4, 2, 1]
CHAIN3 = [32, 16, 8, 4, 2, 1]
CHAIN4 = [2048, 1024, 512, 256, 128, 64, 32, 16, 8, 4, 2, 1]


class Tile:
    __slots__ = ("name", "writer", "readers")
    def __init__(self, name):
        self.name = name; self.writer = None; self.readers = []


class B:
    ENGS = ("vector", "scalar", "tensor", "sync", "gp")

    def __init__(self):
        self.q = []
        self.idx = {e: 0 for e in self.ENGS}
        self.tiles = {}

    def T(self, name):
        if name not in self.tiles:
            self.tiles[name] = Tile(name)
        return self.tiles[name]

    def op(self, eng, emit, reads=(), writes=(), inc=None, force_inc=False):
        if inc is None:
            inc = 16 if eng in ("sync", "gp") else 1
        reads = [self.T(r) for r in reads]
        writes = [self.T(w) for w in writes]
        waits = []
        for t in reads:
            if t.writer is not None:
                waits.append(t.writer)
        for t in writes:
            if t.writer is not None:
                waits.append(t.writer)
            waits.extend(t.readers)
        tok = (eng, self.idx[eng])
        self.idx[eng] += 1
        self.q.append((eng, emit, list(set(waits)), inc, force_inc))
        for t in reads:
            t.readers.append(tok)
        for t in writes:
            t.writer = tok
            t.readers = []
        return tok

    def v(self, emit, reads=(), writes=()):
        return self.op("vector", emit, reads, writes)
    def s(self, emit, reads=(), writes=()):
        return self.op("scalar", emit, reads, writes)
    def t(self, emit, reads=(), writes=()):
        return self.op("tensor", emit, reads, writes)
    def dl(self, emit, reads=(), writes=(), force_inc=False):
        return self.op("sync", emit, reads, writes, force_inc=force_inc)
    def dg(self, emit, reads=(), writes=(), force_inc=False):
        return self.op("gp", emit, reads, writes, force_inc=force_inc)

    def emit_into(self, block, sems, limit=None):
        q = self.q if limit is None else self.q[:limit]
        needed = set()
        for (_, _, waits, _, _) in q:
            needed.update(waits)
        import os as _os4
        dual = {"vector", "tensor", "scalar"} if _os4.environ.get("KDUAL") else set()
        val = {}
        cum = {}
        per_eng = {e: [] for e in self.ENGS}
        idx_in_eng = {e: 0 for e in self.ENGS}
        nth = {e: 0 for e in self.ENGS}
        for (eng, emit, waits, inc, force) in q:
            tok = (eng, idx_in_eng[eng]); idx_in_eng[eng] += 1
            do_inc = force or (tok in needed) or eng in ("sync", "gp") or bool(int(_os4.environ.get("KALLINC", "0")))
            sname = eng
            if do_inc:
                if eng in dual:
                    sname = eng + str(nth[eng] % 2)
                    nth[eng] += 1
                cum[sname] = cum.get(sname, 0) + inc
            else:
                sname = (eng + str((nth[eng] - 1) % 2)) if (eng in dual and nth[eng]) else eng
            val[tok] = (sname, cum.get(sname, 0))
            per_eng[eng].append((emit, waits, inc if do_inc else 0,
                                 (eng + str(nth[eng] % 2 if False else (nth[eng] - 1) % 2)) if False else None))
        waited_vals = {e: set() for e in self.ENGS}
        for (_, _, waits, _, _) in q:
            for t in waits:
                sn, vv = val[t]
                if sn == t[0]:
                    waited_vals[t[0]].add(vv)
        waited_vals = {e: sorted(v) for e, v in waited_vals.items()}
        attr = {"vector": "vector", "scalar": "scalar", "tensor": "tensor",
                "sync": "sync", "gp": "gpsimd"}
        import bisect
        for ename in self.ENGS:
            items = per_eng[ename]
            is_dma_q = ename in ("sync", "gp")
            wl = waited_vals[ename]
            def make(items, ename, is_dma_q, wl):
                def run(engobj):
                    lw = {}
                    protected = 0
                    cnt = 0
                    ni = 0
                    for (emit, waits, inc, _unused) in items:
                        if is_dma_q and wl and inc:
                            i = bisect.bisect_right(wl, cnt) - 1
                            if i >= 0 and wl[i] > protected:
                                engobj.wait_ge(sems[ename], wl[i])
                                protected = wl[i]
                        wm = {}
                        for t in waits:
                            sn, vv = val[t]
                            wm[sn] = max(wm.get(sn, 0), vv)
                        for (sn, v2) in sorted(wm.items()):
                            if lw.get(sn, 0) < v2:
                                engobj.wait_ge(sems[sn], v2)
                                lw[sn] = v2
                        inst = emit(engobj)
                        if inc:
                            if ename in dual:
                                inst.then_inc(sems[ename + str(ni % 2)], inc)
                                ni += 1
                            else:
                                inst.then_inc(sems[ename], inc)
                        cnt += inc
                return run
            getattr(block, attr[ename])(make(items, ename, is_dma_q, wl))


def build_program(TOK=2048, PRE=6144, FT=256, debug=(), limit=None):
    assert TOK % SC == 0 and PRE % (3 * SC) == 0 or PRE == 0
    nc = bass.Bass("TRN2", target_bir_lowering=False)
    NSC = TOK // SC
    NFT = TOK // FT
    QUART = PRE // 3 if PRE else 0

    din = {}
    def dram_in(name, shape, dt=F32):
        din[name] = nc.dram_tensor(name, list(shape), dt, kind="ExternalInput").ap()
        return din[name]

    xb = dram_in("xb", [D, PRE + TOK])
    dram_in("onehot", [128, 4])
    dram_in("tri", [L, L])
    dram_in("maskE", [L, L])
    dram_in("ident32", [128, 128])
    dram_in("wq", [128, NDC * D]); dram_in("wk", [128, NDC * D]); dram_in("wv", [128, NDC * D]); dram_in("wo", [128, NDC * D])
    dram_in("wif", [128, NDC * 2 * H])
    dram_in("bif_rep", [128, 2 * H])
    dram_in("ln1", [128, NDC]); dram_in("ln2", [128, NDC])
    dram_in("wg", [NGC, 128, D]); dram_in("wu", [NGC, 128, D]); dram_in("wd", [NDC, 128, IG])
    dram_in("w1", [NFC, 128, D]); dram_in("b1", [128, NFC])
    dram_in("w2", [NDC, 128, FF]); dram_in("b2", [128, NDC])
    out_t = nc.dram_tensor("out", [D, TOK], F32, kind="ExternalOutput").ap()
    dbg_t = {}
    for name, shape in debug:
        dbg_t[name] = nc.dram_tensor("dbg_" + name, list(shape), F32, kind="ExternalOutput").ap()

    b = B()
    st = ExitStack()
    def sb(name, shape, dt=F32):
        return st.enter_context(nc.sbuf_tensor(name, list(shape), dt))
    def psum(name, shape):
        return st.enter_context(nc.psum_tensor(name, list(shape), F32))

    ST = sb("stream", [128, NDC * TOK])
    HSC = sb("hsc", [128, NDC * SC], BF16)
    AR3 = sb("ar3", [128, NDC * SC + 2 * FT])
    RREP = sb("rrep", [128, 2 * SC])
    AR4 = sb("ar4", [128, 2 * NDC * SC], BF16)
    AR5 = sb("ar5", [128, 4 * NDC * SC], BF16)
    WST = sb("wst", [128, NDC * D], BF16)
    WIF = sb("wif_s", [128, NDC * 2 * H], BF16)
    BIFR = sb("bifr_s", [128, 2 * H])
    LN1 = sb("ln1_s", [128, NDC]); LN2 = sb("ln2_s", [128, NDC])
    B1S = sb("b1s", [128, NFC]); B2S = sb("b2s", [128, NDC])
    UU = sb("uu", [128, 8 * FT])
    GZR = sb("gzr", [128, NSUB * 2 * H])
    ZETA = sb("zeta", [128, NSUB * H])
    LFT = sb("lft", [128, NSUB * H])
    NLFT = sb("nlft", [128, NSUB * H])
    CUMT = sb("cumt", [128, H])
    CROWS = sb("crows", [65, H * L])
    CUMROWS = CROWS[0:1, :]
    BEXPROWS = CROWS[64:65, :]
    CUML = sb("cuml", [1, H])
    WTILD = sb("wtild", [128, H])
    WTB = sb("wtb", [128, H], BF16)
    D0T = sb("d0t", [128, H])
    LAMR = sb("lamr", [128, H])
    APRE = sb("apre", [128, 8 * 128])
    YSCR = sb("yscr", [128, 8])
    ABF = sb("abf", [128, 8 * 128], BF16)
    DNT = sb("dnt", [128, 2 * H])
    SCC = sb("scc", [64, H * DD]); SCN = sb("scn", [64, H])
    SCCB = sb("sccb", [128, H * DD], BF16); SCNB = sb("scnb", [128, H], BF16)
    SNAP = sb("snap", [64, 2 * H * (DD + 1)], BF16)
    YT = sb("yt", [128, D])
    YF = sb("yf", [128, NDC * 128], BF16)
    OH = sb("oh", [128, 4])
    TRI = sb("tri_s", [L, L]); MASKE = sb("maske_s", [L, L])
    ID32 = sb("id32", [128, 128]); IDB = sb("idb", [128, 128], BF16)
    ONESC = sb("onesc", [128, 1], BF16)
    ONESC32 = sb("onesc32", [128, 1])
    ONESR = sb("onesr", [65, 128])
    RAB = sb("rab", [65, SC])
    ROWA = RAB[0:1, :]; ROWB = RAB[64:65, :]
    CFB = sb("cfb", [16, 128])
    CONSTS = sb("consts", [128, 5])

    PB = [psum(f"pb{i}", [128, 512]) for i in range(8)]

    b.dl(lambda e: e.dma_start(out=OH[:], in_=din["onehot"]), writes=["OH"])
    b.dl(lambda e: e.dma_start(out=TRI[:], in_=din["tri"]), writes=["TRI"])
    b.dl(lambda e: e.dma_start(out=MASKE[:], in_=din["maskE"]), writes=["MASKE"])
    b.dl(lambda e: e.dma_start(out=ID32[:], in_=din["ident32"]), writes=["ID32"])
    b.dg(lambda e: e.dma_start(out=IDB[:], in_=din["ident32"]), writes=["IDB"])
    b.dl(lambda e: e.dma_start(out=BIFR[:], in_=din["bif_rep"]), writes=["BIFR"])
    b.dl(lambda e: e.dma_start(out=LN1[:], in_=din["ln1"]), writes=["LN1"])
    b.dl(lambda e: e.dma_start(out=LN2[:], in_=din["ln2"]), writes=["LN2"])
    b.dl(lambda e: e.dma_start(out=B1S[:], in_=din["b1"]), writes=["B1S"])
    b.dl(lambda e: e.dma_start(out=B2S[:], in_=din["b2"]), writes=["B2S"])
    b.v(lambda e: e.tensor_scalar(out=B1S[:], in0=B1S[:], scalar1=256.0, scalar2=1088.5,
                                  op0=ALU.mult, op1=ALU.add), reads=["B1S"], writes=["B1S"])
    b.v(lambda e: e.tensor_scalar(out=B2S[:], in0=B2S[:], scalar1=256.0, scalar2=2048.5,
                                  op0=ALU.mult, op1=ALU.add), reads=["B2S"], writes=["B2S"])
    b.dg(lambda e: e.dma_start(out=WIF[:, :], in_=din["wif"]), writes=["WIF"])
    b.v(lambda e: e.memset(ONESC[:], 1.0), writes=["ONESC"])
    b.v(lambda e: e.memset(ONESC32[:], 1.0), writes=["ONESC32"])
    b.v(lambda e: e.memset(ONESR[:, :], 1.0), writes=["ONESR"])
    for i, val in enumerate([1e-5, 1e-6, -4.25, -4.0, 1e-11]):
        b.v(lambda e, i=i, val=val: e.memset(CONSTS[:, i:i + 1], val), writes=["CONSTS"])
        nc.const_aps.register(F32, val, CONSTS[:, i:i + 1])
    for _p in range(getattr(b, "_pad", 0)):
        b.v(lambda e, _p=_p: e.memset(YSCR[0:1, 4 + _p % 4:5 + _p % 4], 0.0), writes=[f"_pad{_p}"])
    b.v(lambda e: e.memset(SCC[:], 0.0), writes=["SCC"])
    b.v(lambda e: e.memset(SCN[:], 0.0), writes=["SCN"])

    XS = lambda c: AR3[:, c * SC:(c + 1) * SC]
    GAR = AR3[:, NDC * SC:NDC * SC + FT]
    SIL = AR3[:, NDC * SC:NDC * SC + SC]
    GBR = AR3[:, NDC * SC + FT:NDC * SC + 2 * FT]
    XSB = lambda c: AR4[:, c * SC:(c + 1) * SC]
    XSQ = lambda c: AR4[:, NDC * SC + c * SC:NDC * SC + (c + 1) * SC]
    QF = lambda c: AR5[:, c * SC:(c + 1) * SC]
    KFt = lambda c: AR5[:, NDC * SC + c * SC:NDC * SC + (c + 1) * SC]
    KTOK = lambda sub: AR5[:, 2 * NDC * SC + sub * D: 2 * NDC * SC + (sub + 1) * D]
    VTOK = lambda sub: AR5[:, 3 * NDC * SC + sub * D: 3 * NDC * SC + (sub + 1) * D]
    VWT = sb("vwt", [128, D], BF16)
    HQc = lambda fc: AR5[:, fc * FT:(fc + 1) * FT]
    SWTc = lambda g: AR5[:, g * SC:(g + 1) * SC]
    WCH = lambda slot: WST[:, slot * NDC * 128:(slot + 1) * NDC * 128]
    A2 = lambda c: WST[:, 2 * D + c * FT: 2 * D + (c + 1) * FT]
    U = lambda i: UU[:, i * FT:(i + 1) * FT]
    FOUT = UU[:, 5 * FT:6 * FT]

    def ln_finalize(n, rms_only=False):
        b.v(lambda e: e.tensor_scalar_mul(ROWA[0:1, 0:n], PB[6][0:1, 0:n], 1.0 / D),
            reads=["PB6"], writes=["ROWA"])
        b.v(lambda e: e.tensor_scalar_mul(ROWB[0:1, 0:n], PB[6][32:33, 0:n], 1.0 / D),
            reads=["PB6"], writes=["ROWB"])

    def emit_ln(t0, own):
        for c in range(NDC):
            b.dl(lambda e, c=c: e.dma_start(out=XS(c), in_=xb[c * 128:(c + 1) * 128, t0:t0 + SC]),
                 writes=["XS"])
        for c in range(NDC):
            b.dg(lambda e, c=c: e.dma_start(out=XSB(c), in_=xb[c * 128:(c + 1) * 128, t0:t0 + SC]),
                 writes=["XSB"])
        b.s(lambda e: e.activation(AR4[:, NDC * SC:2 * NDC * SC], AR3[:, 0:NDC * SC],
                                   AF.Square, bias=0.0, scale=1.0),
            reads=["XS"], writes=["XSQ"])
        for c in range(NDC):
            b.t(lambda e, c=c: e.matmul(PB[6][0:1, 0:SC], ONESC[:], XSB(c),
                                        start=(c == 0), stop=(c == NDC - 1)),
                reads=["XSB", "ONESC"], writes=["PB6"])
        for c in range(NDC):
            b.t(lambda e, c=c: e.matmul(PB[6][32:33, 0:SC], ONESC[:], XSQ(c),
                                        start=(c == 0), stop=(c == NDC - 1)),
                reads=["XSQ", "ONESC"], writes=["PB6"])
        ln_finalize(SC)
        tl = t0 - PRE
        b.t(lambda e: e.matmul(PB[7][:, 0:SC], ONESR[0:1, :], ROWA[0:1, 0:SC], start=True, stop=True),
            reads=["ROWA", "ONESR"], writes=["PB7"])
        b.t(lambda e: e.matmul(PB[6][:, 0:SC], ONESR[64:65, :], ROWB[0:1, 0:SC], start=True, stop=True),
            reads=["ROWB", "ONESR", "PB6"], writes=["PB6"])
        b.v(lambda e: e.tensor_copy(RREP[:, 0:SC], PB[7][:, 0:SC]), reads=["PB7"], writes=["RREP0"])
        for c in range(NDC):
            b.v(lambda e, c=c: e.tensor_sub(XS(c), XS(c), RREP[:, 0:SC]),
                reads=["RREP0"], writes=["XS"])
        b.s(lambda e: e.activation(RREP[:, SC:2 * SC], RREP[:, 0:SC], AF.Square, bias=0.0, scale=1.0),
            reads=["RREP0"], writes=["RREP1"])
        b.v(lambda e: e.tensor_sub(RREP[:, 0:SC], PB[6][:, 0:SC], RREP[:, SC:2 * SC]),
            reads=["PB6", "RREP1", "RREP0"], writes=["RREP0"])
        if own:
            b.s(lambda e: e.activation(RREP[:, SC:2 * SC], RREP[:, 0:SC], AF.Sqrt,
                                       bias=1e-11, scale=1.000001),
                reads=["RREP0", "RREP1", "CONSTS"], writes=["RREP1"])
            b.v(lambda e: e.reciprocal(RREP[:, SC:2 * SC], RREP[:, SC:2 * SC]),
                reads=["RREP1"], writes=["RREP1"])
            for c in range(NDC):
                b.v(lambda e, c=c: e.tensor_mul(ST[:, c * TOK + tl:c * TOK + tl + SC],
                                                XS(c), RREP[:, SC:2 * SC]),
                    reads=["XS", "RREP1"], writes=["ST"])
        b.s(lambda e: e.activation(RREP[:, 0:SC], RREP[:, 0:SC], AF.Sqrt, bias=1e-5, scale=1.0),
            reads=["RREP0", "RREP1", "CONSTS"], writes=["RREP0"])
        b.v(lambda e: e.reciprocal(RREP[:, 0:SC], RREP[:, 0:SC]), reads=["RREP0"], writes=["RREP0"])
        for c in range(NDC):
            b.v(lambda e, c=c: e.scalar_tensor_tensor(
                HSC[:, c * SC:(c + 1) * SC], XS(c), LN1[:, c:c + 1], RREP[:, 0:SC],
                ALU.mult, ALU.mult),
                reads=["XS", "LN1", "RREP0"], writes=["HSC"])

    def emit_proj(own):
        WSTALL = ("WCH0", "WCH1", "A2")
        def wload(wname):
            b.dg(lambda e, wn=wname: e.dma_start(
                out=WST[:, 0:NDC * D], in_=din[wn]),
                writes=WSTALL)
        if own:
            wload("wq")
            for oc in range(NDC):
                pn = f"PB{oc % 2}"
                for c in range(NDC):
                    b.t(lambda e, oc=oc, c=c: e.matmul(
                        PB[oc % 2][:, 0:SC],
                        WST[:, c * D + oc * 128: c * D + oc * 128 + 128],
                        HSC[:, c * SC:(c + 1) * SC],
                        start=(c == 0), stop=(c == NDC - 1)),
                        reads=["HSC"] + list(WSTALL), writes=[pn])
                b.v(lambda e, oc=oc: e.tensor_copy(QF(oc), PB[oc % 2][:, 0:SC]),
                    reads=[pn], writes=["QF"])
        wload("wk")
        if own:
            for oc in range(NDC):
                pn = f"PB{oc % 2}"
                for c in range(NDC):
                    b.t(lambda e, oc=oc, c=c: e.matmul(
                        PB[oc % 2][:, 0:SC],
                        WST[:, c * D + oc * 128: c * D + oc * 128 + 128],
                        HSC[:, c * SC:(c + 1) * SC],
                        start=(c == 0), stop=(c == NDC - 1)),
                        reads=["HSC"] + list(WSTALL), writes=[pn])
                b.v(lambda e, oc=oc: e.tensor_copy(KFt(oc), PB[oc % 2][:, 0:SC]),
                    reads=[pn], writes=["KF"])
        for sub in range(NSUB):
            for half in range(2):
                pn = f"PB{half}"
                for c in range(NDC):
                    b.t(lambda e, sub=sub, c=c, half=half: e.matmul(
                        PB[half][:, 0:512],
                        HSC[:, c * SC + sub * 128: c * SC + sub * 128 + 128],
                        WST[:, c * D + half * 512: c * D + half * 512 + 512],
                        start=(c == 0), stop=(c == NDC - 1)),
                        reads=["HSC"] + list(WSTALL), writes=[pn])
                b.v(lambda e, sub=sub, half=half: e.tensor_copy(
                    KTOK(sub)[:, half * 512:(half + 1) * 512], PB[half][:, 0:512]),
                    reads=[pn], writes=["KTOK"])
        wload("wv")
        for sub in range(NSUB):
            for half in range(2):
                pn = f"PB{half}"
                for c in range(NDC):
                    b.t(lambda e, sub=sub, c=c, half=half: e.matmul(
                        PB[half][:, 0:512],
                        HSC[:, c * SC + sub * 128: c * SC + sub * 128 + 128],
                        WST[:, c * D + half * 512: c * D + half * 512 + 512],
                        start=(c == 0), stop=(c == NDC - 1)),
                        reads=["HSC"] + list(WSTALL), writes=[pn])
                b.v(lambda e, sub=sub, half=half: e.tensor_copy(
                    VTOK(sub)[:, half * 512:(half + 1) * 512], PB[half][:, 0:512]),
                    reads=[pn], writes=["VTOK"])
        for sub in range(NSUB):
            for c in range(NDC):
                b.t(lambda e, sub=sub, c=c: e.matmul(
                    PB[2][:, 0:2 * H],
                    HSC[:, c * SC + sub * 128: c * SC + sub * 128 + 128],
                    WIF[:, c * 2 * H:(c + 1) * 2 * H],
                    start=(c == 0), stop=(c == NDC - 1)),
                    reads=["HSC", "WIF"], writes=["PB2"])
            b.v(lambda e, sub=sub: e.tensor_add(GZR[:, sub * 2 * H:(sub + 1) * 2 * H],
                                                PB[2][:, 0:2 * H], BIFR[:]),
                reads=["PB2", "BIFR"], writes=["GZR"])
            b.v(lambda e, sub=sub: e.tensor_scalar(
                out=ZETA[:, sub * H:(sub + 1) * H],
                in0=GZR[:, sub * 2 * H: sub * 2 * H + H],
                scalar1=-10.0, scalar2=10.0, op0=ALU.max, op1=ALU.min),
                reads=["GZR"], writes=["ZETA"])
            b.s(lambda e, sub=sub: e.activation(LFT[:, sub * H:(sub + 1) * H],
                                                GZR[:, sub * 2 * H + H: sub * 2 * H + 2 * H],
                                                AF.Sigmoid, bias=0.0, scale=1.0),
                reads=["GZR"], writes=["LFT"])
            b.v(lambda e, sub=sub: e.tensor_scalar_max(LFT[:, sub * H:(sub + 1) * H],
                                                       LFT[:, sub * H:(sub + 1) * H], 1e-30),
                reads=["LFT"], writes=["LFT"])
            b.s(lambda e, sub=sub: e.activation(LFT[:, sub * H:(sub + 1) * H],
                                                LFT[:, sub * H:(sub + 1) * H],
                                                AF.Ln, bias=0.0, scale=1.0),
                reads=["LFT"], writes=["LFT"])
            b.v(lambda e, sub=sub: e.tensor_scalar_mul(NLFT[:, sub * H:(sub + 1) * H],
                                                       LFT[:, sub * H:(sub + 1) * H], -1.0),
                reads=["LFT"], writes=["NLFT"])
        if own:
            wload("wo")

    def grpoff(h):
        return (h // 8) * 16

    def emit_chunk(own, sub, tglob=0):
        so = sub * H
        toff = sub * 128
        b.v(lambda e: e.tensor_copy(SCCB[0:64, :], SCC[:]), reads=["SCC"], writes=["SCCB"])
        b.v(lambda e: e.tensor_copy(SCNB[0:64, :], SCN[:]), reads=["SCN"], writes=["SCNB"])
        b.dg(lambda e: e.dma_start(out=SCCB[64:128, :], in_=SCC[:]), reads=["SCC"], writes=["SCCBH"])
        b.dg(lambda e: e.dma_start(out=SCNB[64:128, :], in_=SCN[:]), reads=["SCN"], writes=["SCNBH"])
        b.t(lambda e: e.matmul(PB[6][0:128, 0:H], TRI[:], LFT[:, so:so + H],
                               start=True, stop=True),
            reads=["TRI", "LFT"], writes=["PB6"])
        b.t(lambda e: e.matmul(PB[6][0:1, 16:16 + H], ONESC32[:], LFT[:, so:so + H],
                               start=True, stop=True),
            reads=["LFT", "ONESC32"], writes=["PB6"])
        b.v(lambda e: e.tensor_copy(CUMT[:], PB[6][0:128, 0:H]), reads=["PB6"], writes=["CUMT"])
        b.v(lambda e: e.tensor_copy(CUML[:], PB[6][0:1, 16:16 + H]), reads=["PB6"], writes=["CUML"])
        b.t(lambda e: e.matmul(PB[6][:, 16:16 + H], ONESR[0:1, :], CUML[:], start=True, stop=True),
            reads=["CUML", "ONESR"], writes=["PB6"])
        b.s(lambda e: e.activation(LAMR[:], PB[6][:, 16:16 + H], AF.Exp, bias=0.0, scale=1.0),
            reads=["PB6"], writes=["LAMR"])
        b.v(lambda e: e.tensor_sub(WTILD[:], ZETA[:, so:so + H], CUMT[:]),
            reads=["ZETA", "CUMT"], writes=["WTILDp"])
        b.v(lambda e: e.tensor_add(WTILD[:], WTILD[:], PB[6][:, 16:16 + H]),
            reads=["WTILDp", "PB6"], writes=["WTILDp"])
        b.s(lambda e: e.activation(WTILD[:], WTILD[:], AF.Exp, bias=0.0, scale=1.0),
            reads=["WTILDp"], writes=["WTILD"])
        b.v(lambda e: e.tensor_copy(WTB[:], WTILD[:]), reads=["WTILD"], writes=["WTB"])
        for h in range(H):
            b.v(lambda e, h=h, sub=sub: e.tensor_single_scalar(
                VWT[:, h * 64:(h + 1) * 64], VTOK(sub)[:, h * 64:h * 64 + 64],
                WTILD[:, h:h + 1], ALU.mult),
                reads=["VTOK", "WTILD", "PB7"], writes=[f"VWT{h}"])
        if own:
            b.s(lambda e: e.activation(D0T[:], CUMT[:], AF.Exp, bias=0.0, scale=1.0),
                reads=["CUMT"], writes=["D0T"])
            b.t(lambda e: e.matmul(PB[6][0:H, 128:256], LFT[:, so:so + H], TRI[:],
                                   start=True, stop=True),
                reads=["LFT", "TRI"], writes=["PB6"])
            b.v(lambda e: e.tensor_copy(CFB[:], PB[6][0:H, 128:256]), reads=["PB6"], writes=["CFB"])
            b.dl(lambda e: e.dma_start(out=CUMROWS, in_=CFB[:]),
                 reads=["CFB"], writes=["CUMROWS"])
            b.t(lambda e: e.matmul(PB[6][0:H, 128:256], ZETA[:, so:so + H], ID32[:],
                                   start=True, stop=False),
                reads=["ZETA", "ID32", "CFB"], writes=["PB6"])
            b.t(lambda e: e.matmul(PB[6][0:H, 128:256], NLFT[:, so:so + H], TRI[:],
                                   start=False, stop=True),
                reads=["NLFT", "TRI"], writes=["PB6"])
            b.v(lambda e: e.tensor_copy(CFB[:], PB[6][0:H, 128:256]), reads=["PB6"], writes=["CFB"])
            b.dl(lambda e: e.dma_start(out=BEXPROWS, in_=CFB[:]),
                 reads=["CFB"], writes=["BEXPROWS"])
            for grp in range(2):
                hs = list(range(grp * 8, grp * 8 + 8))
                for h in hs:
                    c64 = (64 * h) % 128
                    fc = (64 * h) // 128
                    sb_bank = PB[(h % 8) // 4]
                    sb_off = (h % 4) * 128
                    b.t(lambda e, h=h, c64=c64, fc=fc, sb_bank=sb_bank, sb_off=sb_off: e.matmul(
                        sb_bank[:, sb_off:sb_off + 128],
                        KFt(fc)[c64:c64 + 64, toff:toff + 128],
                        QF(fc)[c64:c64 + 64, toff:toff + 128],
                        start=True, stop=True),
                        reads=["KF", "QF", f"ABFr{h % 8}"], writes=[f"PB{(h % 8) // 4}"])
                for h in hs:
                    eb_bank = PB[2 + (h % 8) // 4]
                    eb_off = (h % 4) * 128
                    b.t(lambda e, h=h, eb_bank=eb_bank, eb_off=eb_off: e.matmul(
                        eb_bank[:, eb_off:eb_off + 128],
                        ONESR[0:1, :], CUMROWS[0:1, h * 128:(h + 1) * 128],
                        start=True, stop=False),
                        reads=["CUMROWS", f"APREr{h % 8}"], writes=[f"PB{2 + (h % 8) // 4}"])
                    b.t(lambda e, h=h, eb_bank=eb_bank, eb_off=eb_off: e.matmul(
                        eb_bank[:, eb_off:eb_off + 128],
                        BEXPROWS[0:1, h * 128:(h + 1) * 128], ONESR[64:65, :],
                        start=False, stop=True),
                        reads=["BEXPROWS"], writes=[f"PB{2 + (h % 8) // 4}"])
                for h in hs:
                    eb_bank = PB[2 + (h % 8) // 4]
                    eb_off = (h % 4) * 128
                    b.v(lambda e, h=h, eb_bank=eb_bank, eb_off=eb_off: e.scalar_tensor_tensor(
                        APRE[:, (h % 8) * 128:(h % 8) * 128 + 128],
                        eb_bank[:, eb_off:eb_off + 128], 11.0, MASKE[:], ALU.min, ALU.add),
                        reads=[f"PB{2 + (h % 8) // 4}", "MASKE"], writes=[f"APRE{h % 8}"])
                for h in hs:
                    b.s(lambda e, h=h: e.activation(
                        APRE[:, (h % 8) * 128:(h % 8) * 128 + 128],
                        APRE[:, (h % 8) * 128:(h % 8) * 128 + 128],
                        AF.Exp, bias=0.0, scale=1.0),
                        reads=[f"APRE{h % 8}"], writes=[f"APRE{h % 8}", f"APREr{h % 8}"])
                for h in hs:
                    sb_bank = PB[(h % 8) // 4]
                    sb_off = (h % 4) * 128
                    b.v(lambda e, h=h, sb_bank=sb_bank, sb_off=sb_off: e.tensor_mul(
                        ABF[:, (h % 8) * 128:(h % 8) * 128 + 128],
                        APRE[:, (h % 8) * 128:(h % 8) * 128 + 128],
                        sb_bank[:, sb_off:sb_off + 128]),
                        reads=[f"APRE{h % 8}", f"PB{(h % 8) // 4}"], writes=[f"ABF{h % 8}", f"ABFr{h % 8}"])
                for h in hs:
                    b.t(lambda e, h=h, sub=sub: e.matmul(
                        PB[4][:, (h % 8) * 64:(h % 8) * 64 + 64],
                        ABF[:, (h % 8) * 128:(h % 8) * 128 + 128],
                        VTOK(sub)[:, h * 64:h * 64 + 64],
                        start=True, stop=True),
                        reads=[f"ABF{h % 8}", "VTOK", f"YTr{h}"], writes=["PB4"])
                    b.t(lambda e, h=h, grp=grp: e.matmul(
                        PB[6][0:128, 256 + grpoff(h) + (h % 8):257 + grpoff(h) + (h % 8)],
                        ABF[:, (h % 8) * 128:(h % 8) * 128 + 128], ONESC[:],
                        start=True, stop=True),
                        reads=[f"ABF{h % 8}", "ONESC", f"DNTg{grp}"], writes=["PB6"])
                for h in hs:
                    c64 = (64 * h) % 128
                    fc = (64 * h) // 128
                    b.t(lambda e, h=h, c64=c64, fc=fc: e.matmul(
                        PB[5][:, (h % 8) * 64:(h % 8) * 64 + 64],
                        QF(fc)[c64:c64 + 64, toff:toff + 128],
                        SCCB[c64:c64 + 64, h * 64:(h + 1) * 64],
                        start=True, stop=True),
                        reads=["QF", "SCCB", "SCCBH", f"YTr{h}"], writes=["PB5"])
                    b.t(lambda e, h=h, c64=c64, fc=fc, grp=grp: e.matmul(
                        PB[6][0:128, 288 + grpoff(h) + (h % 8):289 + grpoff(h) + (h % 8)],
                        QF(fc)[c64:c64 + 64, toff:toff + 128],
                        SCNB[c64:c64 + 64, h:h + 1],
                        start=True, stop=True),
                        reads=["QF", "SCNB", "SCNBH", f"DNTg{grp}"], writes=["PB6"])
                b.v(lambda e, grp=grp: e.tensor_mul(
                    DNT[:, grp * 8:grp * 8 + 8],
                    PB[6][0:128, 288 + grp * 16:288 + grp * 16 + 8],
                    D0T[:, grp * 8:grp * 8 + 8]),
                    reads=["PB6", "D0T"], writes=[f"DNTg{grp}"])
                b.v(lambda e, grp=grp: e.tensor_add(
                    DNT[:, grp * 8:grp * 8 + 8], DNT[:, grp * 8:grp * 8 + 8],
                    PB[6][0:128, 256 + grp * 16:256 + grp * 16 + 8]),
                    reads=["PB6", f"DNTg{grp}"], writes=[f"DNTg{grp}"])
                b.v(lambda e, grp=grp: e.tensor_scalar(
                    out=DNT[:, 16 + grp * 8:16 + grp * 8 + 8],
                    in0=DNT[:, grp * 8:grp * 8 + 8],
                    scalar1=-1.0, scalar2=1.0, op0=ALU.mult, op1=ALU.max),
                    reads=[f"DNTg{grp}"], writes=[f"DNTn{grp}"])
                b.v(lambda e, grp=grp: e.tensor_max(
                    DNT[:, grp * 8:grp * 8 + 8], DNT[:, grp * 8:grp * 8 + 8],
                    DNT[:, 16 + grp * 8:16 + grp * 8 + 8]),
                    reads=[f"DNTn{grp}", f"DNTg{grp}"], writes=[f"DNTg{grp}"])
                b.v(lambda e, grp=grp: e.reciprocal(
                    DNT[:, 16 + grp * 8:16 + grp * 8 + 8], DNT[:, grp * 8:grp * 8 + 8]),
                    reads=[f"DNTg{grp}"], writes=[f"DNTr{grp}"])
                for h in hs:
                    b.v(lambda e, h=h: e.tensor_single_scalar(
                        YT[:, h * 64:(h + 1) * 64],
                        PB[5][:, (h % 8) * 64:(h % 8) * 64 + 64], D0T[:, h:h + 1], ALU.mult),
                        reads=["PB5", "D0T"], writes=[f"YT{h}"])
                    b.v(lambda e, h=h: e.scalar_tensor_tensor(
                        YT[:, h * 64:(h + 1) * 64], PB[4][:, (h % 8) * 64:(h % 8) * 64 + 64],
                        1.0, YT[:, h * 64:(h + 1) * 64], ALU.mult, ALU.add),
                        reads=["PB4", f"YT{h}"], writes=[f"YT{h}"])
                    b.v(lambda e, h=h, grp=grp: e.tensor_single_scalar(
                        YT[:, h * 64:(h + 1) * 64], YT[:, h * 64:(h + 1) * 64],
                        DNT[:, 16 + h:16 + h + 1], ALU.mult),
                        reads=[f"YT{h}", f"DNTr{grp}"], writes=[f"YT{h}", f"YTr{h}"])
            for c in range(NDC):
                b.t(lambda e, c=c: e.transpose(PB[6][:, 320:448], YT[:, c * 128:(c + 1) * 128],
                                               ID32[:]),
                    reads=[f"YT{2 * c}", f"YT{2 * c + 1}", "ID32", "YFp"], writes=["PB6"])
                b.v(lambda e, c=c: e.tensor_copy(YF[:, c * 128:(c + 1) * 128], PB[6][:, 320:448]),
                    reads=["PB6"], writes=["YFp"])
            for oc in range(NDC):
                for c in range(NDC):
                    b.t(lambda e, oc=oc, c=c: e.matmul(
                        PB[0][:, 0:128],
                        WST[:, c * D + oc * 128: c * D + oc * 128 + 128],
                        YF[:, c * 128:(c + 1) * 128],
                        start=(c == 0), stop=(c == NDC - 1)),
                        reads=["YFp", "WCH0", "WCH1", "A2"], writes=["PB0"])
                b.v(lambda e, oc=oc, toffv=sub * 128: e.tensor_add(
                    ST[:, oc * TOK + tglob + toffv: oc * TOK + tglob + toffv + 128],
                    ST[:, oc * TOK + tglob + toffv: oc * TOK + tglob + toffv + 128],
                    PB[0][:, 0:128]),
                    reads=["PB0"], writes=["ST"])
        for grp in range(2):
            hs = list(range(grp * 8, grp * 8 + 8))
            for h in hs:
                b.t(lambda e, h=h, sub=sub: e.matmul(
                    PB[7][0:64, (h % 8) * 64:(h % 8) * 64 + 64],
                    KTOK(sub)[:, h * 64:h * 64 + 64], VWT[:, h * 64:(h + 1) * 64],
                    start=True, stop=True),
                    reads=["KTOK", f"VWT{h}", f"SCCd{h}"], writes=["PB7"])
                b.t(lambda e, h=h, sub=sub, grp=grp: e.matmul(
                    PB[6][0:64, 448 + grpoff(h) + (h % 8):449 + grpoff(h) + (h % 8)],
                    KTOK(sub)[:, h * 64:h * 64 + 64], WTB[:, h:h + 1],
                    start=True, stop=True),
                    reads=["KTOK", "WTB", f"SCNd{h}"], writes=["PB6"])
            for h in hs:
                b.v(lambda e, h=h: e.scalar_tensor_tensor(
                    SCC[:, h * 64:(h + 1) * 64], SCC[:, h * 64:(h + 1) * 64],
                    LAMR[0:64, h:h + 1], PB[7][0:64, (h % 8) * 64:(h % 8) * 64 + 64],
                    ALU.mult, ALU.add),
                    reads=["PB7", "LAMR", "SCCB", "SCCBH"], writes=[f"SCCd{h}"])
                b.v(lambda e, h=h, grp=grp: e.scalar_tensor_tensor(
                    SCN[:, h:h + 1], SCN[:, h:h + 1], LAMR[0:64, h:h + 1],
                    PB[6][0:64, 448 + grpoff(h) + (h % 8):449 + grpoff(h) + (h % 8)],
                    ALU.mult, ALU.add),
                    reads=["PB6", "LAMR", "SCNB", "SCNBH"], writes=[f"SCNd{h}"])
        b.v(lambda e: e.memset(YSCR[0:1, 0:1], 0.0),
            reads=[f"SCCd{h}" for h in range(H)] + [f"SCNd{h}" for h in range(H)],
            writes=["SCC", "SCN"])

    if PRE:
        for q in range(3):
            if q >= 1:
                base = (q - 1) * H * (DD + 1)
                b.v(lambda e, base=base: e.tensor_copy(
                    SNAP[:, base: base + H * DD], SCC[:]),
                    reads=["SCC"], writes=["SNAP"])
                b.v(lambda e, base=base: e.tensor_copy(
                    SNAP[:, base + H * DD: base + H * (DD + 1)], SCN[:]),
                    reads=["SCN"], writes=["SNAP"])
            for sci in range(QUART // SC):
                emit_ln(q * QUART + sci * SC, own=False)
                emit_proj(own=False)
                for sub in range(NSUB):
                    emit_chunk(own=False, sub=sub)
        b.v(lambda e: e.tensor_single_scalar(SCC[:], SCC[:], OH[0:64, 3:4], ALU.mult),
            reads=["OH"], writes=["SCC"])
        b.v(lambda e: e.tensor_single_scalar(SCN[:], SCN[:], OH[0:64, 3:4], ALU.mult),
            reads=["OH"], writes=["SCN"])
        for q in (1, 2):
            base = (q - 1) * H * (DD + 1)
            b.v(lambda e, base=base, q=q: e.scalar_tensor_tensor(
                SCC[:], SNAP[:, base:base + H * DD], OH[0:64, q:q + 1], SCC[:],
                ALU.mult, ALU.add),
                reads=["SNAP", "OH"], writes=["SCC"])
            b.v(lambda e, base=base, q=q: e.scalar_tensor_tensor(
                SCN[:], SNAP[:, base + H * DD:base + H * (DD + 1)], OH[0:64, q:q + 1], SCN[:],
                ALU.mult, ALU.add),
                reads=["SNAP", "OH"], writes=["SCN"])

    for s in range(NSC):
        emit_ln(PRE + s * SC, own=True)
        emit_proj(own=True)
        for sub in range(NSUB):
            emit_chunk(own=True, sub=sub, tglob=s * SC)

    b.v(lambda e: e.memset(YSCR[0:1, 1:2], 0.0),
        reads=["QF", "KF", "KTOK", "VTOK"], writes=["SWT"])

    for s in range(NSC):
        tl = s * SC
        for c in range(NDC):
            b.s(lambda e, c=c, tl=tl: e.activation(
                XSQ(c), ST[:, c * TOK + tl:c * TOK + tl + SC],
                AF.Square, bias=0.0, scale=1.0),
                reads=["ST"], writes=["XSQ"])
        for c in range(NDC):
            b.t(lambda e, c=c: e.matmul(PB[6][32:33, 0:SC], ONESC[:], XSQ(c),
                                        start=(c == 0), stop=(c == NDC - 1)),
                reads=["XSQ", "ONESC"], writes=["PB6"])
        b.v(lambda e: e.tensor_scalar_mul(ROWB[0:1, 0:SC], PB[6][32:33, 0:SC], 1.0 / D),
            reads=["PB6"], writes=["ROWB"])
        b.s(lambda e: e.activation(ROWB[0:1, 0:SC], ROWB[0:1, 0:SC], AF.Sqrt, bias=1e-6, scale=1.0),
            reads=["ROWB", "CONSTS"], writes=["ROWB"])
        b.v(lambda e: e.reciprocal(ROWB[0:1, 0:SC], ROWB[0:1, 0:SC]), reads=["ROWB"], writes=["ROWB"])
        b.t(lambda e: e.matmul(PB[7][:, 0:SC], ONESR[64:65, :], ROWB[0:1, 0:SC], start=True, stop=True),
            reads=["ROWB", "ONESR"], writes=["PB7"])
        b.v(lambda e: e.tensor_copy(RREP[:, 0:SC], PB[7][:, 0:SC]), reads=["PB7"], writes=["RREP0"])
        for c in range(NDC):
            b.v(lambda e, c=c, tl=tl: e.scalar_tensor_tensor(
                HSC[:, c * SC:(c + 1) * SC],
                ST[:, c * TOK + tl:c * TOK + tl + SC], LN2[:, c:c + 1], RREP[:, 0:SC],
                ALU.mult, ALU.mult),
                reads=["ST", "LN2", "RREP0"], writes=["HSC"])
        for g in range(NGC):
            b.dg(lambda e, g=g: e.dma_start(out=WCH(0)[:, 0:NDC * 128], in_=din["wg"][g]),
                 writes=["WCH0"])
            b.dg(lambda e, g=g: e.dma_start(out=WCH(1)[:, 0:NDC * 128], in_=din["wu"][g]),
                 writes=["WCH1"])
            for c in range(NDC):
                b.t(lambda e, g=g, c=c: e.matmul(
                    PB[0][:, 0:SC], WCH(0)[:, c * 128:(c + 1) * 128],
                    HSC[:, c * SC:(c + 1) * SC],
                    start=(c == 0), stop=(c == NDC - 1)),
                    reads=["WCH0", "HSC"], writes=["PB0"])
            for c in range(NDC):
                b.t(lambda e, g=g, c=c: e.matmul(
                    PB[1][:, 0:SC], WCH(1)[:, c * 128:(c + 1) * 128],
                    HSC[:, c * SC:(c + 1) * SC],
                    start=(c == 0), stop=(c == NDC - 1)),
                    reads=["WCH1", "HSC"], writes=["PB1"])
            b.s(lambda e: e.activation(SIL[:], PB[0][:, 0:SC], AF.Silu, bias=0.0, scale=1.0),
                reads=["PB0"], writes=["SIL"])
            b.v(lambda e, g=g: e.tensor_mul(SWTc(g), SIL[:], PB[1][:, 0:SC]),
                reads=["SIL", "PB1"], writes=["SWT"])
        for oc in range(NDC):
            for gb in range((NGC + 7) // 8):
                ng = min(8, NGC - gb * 8)
                slot = gb % 2
                b.dg(lambda e, oc=oc, gb=gb, ng=ng, slot=slot: e.dma_start(
                    out=WCH(slot)[:, 0:ng * 128],
                    in_=din["wd"][oc][:, gb * 1024:gb * 1024 + ng * 128]),
                    writes=[f"WCH{slot}"])
                for gi in range(ng):
                    g = gb * 8 + gi
                    b.t(lambda e, g=g, gi=gi, slot=slot: e.matmul(
                        PB[3][:, 0:SC], WCH(slot)[:, gi * 128:(gi + 1) * 128], SWTc(g),
                        start=(g == 0), stop=(g == NGC - 1)),
                        reads=[f"WCH{slot}", "SWT"], writes=["PB3"])
            b.v(lambda e, oc=oc, tl=tl: e.tensor_add(
                ST[:, oc * TOK + tl:oc * TOK + tl + SC],
                ST[:, oc * TOK + tl:oc * TOK + tl + SC], PB[3][:, 0:SC]),
                reads=["PB3"], writes=["ST"])
        for c in range(NDC):
            b.dl(lambda e, c=c, tl=tl: e.dma_start(
                out=XS(c), in_=xb[c * 128:(c + 1) * 128, PRE + tl:PRE + tl + SC]),
                writes=["XS"])
        for c in range(NDC):
            b.v(lambda e, c=c, tl=tl: e.tensor_add(
                ST[:, c * TOK + tl:c * TOK + tl + SC],
                ST[:, c * TOK + tl:c * TOK + tl + SC], XS(c)),
                reads=["XS"], writes=["ST"])

    b.v(lambda e: e.memset(YSCR[0:1, 2:3], 0.0), reads=["SWT"], writes=["HQ"])

    def chain(levels, rmain, rname, rtmp, tmpname, keeps):
        for c in levels:
            b.v(lambda e, c=c: e.tensor_scalar(out=rtmp, in0=rmain, scalar1=float(c),
                                               scalar2=float(c), op0=ALU.is_ge, op1=ALU.mult),
                reads=[rname], writes=[tmpname])
            b.v(lambda e: e.tensor_sub(rmain, rmain, rtmp),
                reads=[tmpname, rname], writes=[rname])
            if c in keeps:
                dst, dname = keeps[c]
                b.v(lambda e, dst=dst: e.tensor_copy(dst, rmain), reads=[rname], writes=[dname])

    for ft in range(NFT):
        tl = ft * FT
        for c in range(NDC):
            b.s(lambda e, c=c, tl=tl: e.activation(
                XSQ(c)[:, 0:FT], ST[:, c * TOK + tl:c * TOK + tl + FT],
                AF.Square, bias=0.0, scale=1.0), reads=["ST"], writes=["XSQ"])
            b.v(lambda e, c=c, tl=tl: e.tensor_copy(
                XSB(c)[:, 0:FT], ST[:, c * TOK + tl:c * TOK + tl + FT]),
                reads=["ST"], writes=["XSB"])
        for c in range(NDC):
            b.t(lambda e, c=c: e.matmul(PB[6][0:1, 0:FT], ONESC[:], XSB(c)[:, 0:FT],
                                        start=(c == 0), stop=(c == NDC - 1)),
                reads=["XSB", "ONESC"], writes=["PB6"])
        for c in range(NDC):
            b.t(lambda e, c=c: e.matmul(PB[6][32:33, 0:FT], ONESC[:], XSQ(c)[:, 0:FT],
                                        start=(c == 0), stop=(c == NDC - 1)),
                reads=["XSQ", "ONESC"], writes=["PB6"])
        b.v(lambda e: e.tensor_scalar_mul(ROWA[0:1, 0:FT], PB[6][0:1, 0:FT], 1.0 / D),
            reads=["PB6"], writes=["ROWA"])
        b.v(lambda e: e.tensor_scalar_mul(ROWB[0:1, 0:FT], PB[6][32:33, 0:FT], 1.0 / D),
            reads=["PB6"], writes=["ROWB"])
        b.t(lambda e: e.matmul(PB[7][:, 0:FT], ONESR[0:1, :], ROWA[0:1, 0:FT], start=True, stop=True),
            reads=["ROWA", "ONESR"], writes=["PB7"])
        b.t(lambda e: e.matmul(PB[6][:, 0:FT], ONESR[64:65, :], ROWB[0:1, 0:FT], start=True, stop=True),
            reads=["ROWB", "ONESR", "PB6"], writes=["PB6"])
        b.v(lambda e: e.tensor_copy(RREP[:, 0:FT], PB[7][:, 0:FT]), reads=["PB7"], writes=["RREP0"])
        for c in range(NDC):
            b.v(lambda e, c=c, tl=tl: e.tensor_sub(
                XS(c)[:, 0:FT], ST[:, c * TOK + tl:c * TOK + tl + FT], RREP[:, 0:FT]),
                reads=["ST", "RREP0"], writes=["XS"])
        b.s(lambda e: e.activation(RREP[:, SC:SC + FT], RREP[:, 0:FT], AF.Square, bias=0.0, scale=1.0),
            reads=["RREP0"], writes=["RREP1"])
        b.v(lambda e: e.tensor_sub(RREP[:, 0:FT], PB[6][:, 0:FT], RREP[:, SC:SC + FT]),
            reads=["PB6", "RREP1", "RREP0"], writes=["RREP0"])
        b.s(lambda e: e.activation(RREP[:, 0:FT], RREP[:, 0:FT], AF.Sqrt, bias=1e-5, scale=1.0),
            reads=["RREP0", "CONSTS"], writes=["RREP0"])
        b.v(lambda e: e.reciprocal(RREP[:, 0:FT], RREP[:, 0:FT]), reads=["RREP0"], writes=["RREP0"])
        for c in range(NDC):
            b.v(lambda e, c=c: e.tensor_mul(A2(c), XS(c)[:, 0:FT], RREP[:, 0:FT]),
                reads=["XS", "RREP0"], writes=["A2"])
        for fcid in range(NFC):
            slot = fcid % 2
            b.dg(lambda e, fcid=fcid, slot=slot: e.dma_start(
                out=WCH(slot)[:, 0:NDC * 128], in_=din["w1"][fcid]),
                writes=[f"WCH{slot}"])
            for c in range(NDC):
                b.t(lambda e, fcid=fcid, c=c, slot=slot: e.matmul(
                    PB[slot][:, 0:FT], WCH(slot)[:, c * 128:(c + 1) * 128], A2(c),
                    start=(c == 0), stop=(c == NDC - 1)),
                    reads=[f"WCH{slot}", "A2"], writes=[f"PB{slot}"])
            b.v(lambda e, fcid=fcid, slot=slot: e.tensor_scalar(
                out=U(0), in0=PB[slot][:, 0:FT], scalar1=256.0,
                scalar2=B1S[:, fcid:fcid + 1], op0=ALU.mult, op1=ALU.add),
                reads=[f"PB{slot}", "B1S"], writes=["Uu"])
            b.v(lambda e: e.tensor_scalar(out=U(1), in0=U(0), scalar1=0.5, scalar2=2176.5,
                                          op0=ALU.max, op1=ALU.min),
                reads=["Uu"], writes=["Uc"])
            b.v(lambda e: e.tensor_copy(U(2), U(1)), reads=["Uc"], writes=["Ur"])
            chain(CHAIN1, U(2), "Ur", U(3), "Utmp", {64: (U(4), "Ur6")})
            b.v(lambda e: e.tensor_sub(U(5), U(1), U(4)), reads=["Uc", "Ur6"], writes=["Uiarg", "FOUT"])
            b.v(lambda e: e.tensor_sub(U(4), U(4), U(2)), reads=["Ur6", "Ur"], writes=["Ufr64"])
            b.s(lambda e: e.activation(GAR, U(5), AF.Gelu, bias=-4.25, scale=1.0 / 256.0),
                reads=["Uiarg", "CONSTS", "SIL"], writes=["Uga"])
            b.s(lambda e: e.activation(GBR, U(5), AF.Gelu, bias=-4.0, scale=1.0 / 256.0),
                reads=["Uiarg", "CONSTS", "SIL"], writes=["Ugb"])
            b.v(lambda e: e.tensor_sub(GBR, GBR, GAR), reads=["Ugb", "Uga"], writes=["Ugb"])
            b.v(lambda e: e.tensor_mul(U(4), U(4), GBR), reads=["Ufr64", "Ugb"], writes=["Ufr64"])
            b.v(lambda e: e.tensor_sub(U(6), U(1), U(2)), reads=["Uc", "Ur"], writes=["Uxq"])
            b.v(lambda e: e.tensor_scalar_add(U(6), U(6), -1088.0), reads=["Uxq"], writes=["Uxq"])
            b.v(lambda e: e.tensor_scalar(out=U(7), in0=U(6), scalar1=0.0, scalar2=0.5,
                                          op0=ALU.max, op1=ALU.add),
                reads=["Uxq"], writes=["Uz"])
            b.v(lambda e: e.scalar_tensor_tensor(U(3), GAR, -256.0, U(7), ALU.mult, ALU.add),
                reads=["Uga", "Uz"], writes=["Uv"])
            b.v(lambda e: e.scalar_tensor_tensor(U(3), U(4), -4.0, U(3), ALU.mult, ALU.add),
                reads=["Ufr64", "Uv"], writes=["Uv"])
            b.v(lambda e: e.tensor_copy(U(1), U(3)), reads=["Uv"], writes=["Uvk"])
            chain(CHAIN3, U(3), "Uv", U(2), "Utmp", {})
            b.v(lambda e: e.tensor_sub(U(1), U(1), U(3)), reads=["Uvk", "Uv"], writes=["Uvk"])
            b.v(lambda e: e.tensor_sub(U(1), U(7), U(1)), reads=["Uz", "Uvk"], writes=["Um05"])
            b.v(lambda e: e.tensor_scalar(out=U(2), in0=U(1), scalar1=1.0 / 256.0,
                                          scalar2=-0.001953125, op0=ALU.mult, op1=ALU.add),
                reads=["Um05"], writes=["Uhq"])
            b.v(lambda e: e.tensor_scalar(out=U(3), in0=U(0), scalar1=1.0 / 256.0,
                                          scalar2=-4.251953125, op0=ALU.mult, op1=ALU.add),
                reads=["Uu"], writes=["Uha"])
            b.v(lambda e: e.tensor_single_scalar(U(4), U(0), 2113.0, ALU.is_ge),
                reads=["Uu"], writes=["Umhi"])
            b.v(lambda e: e.tensor_sub(U(3), U(3), U(2)), reads=["Uha", "Uhq"], writes=["Uha"])
            b.v(lambda e: e.tensor_mul(U(3), U(3), U(4)), reads=["Uha", "Umhi"], writes=["Uha"])
            b.v(lambda e: e.tensor_add(U(5), U(2), U(3)), reads=["Uhq", "Uha"], writes=["Usel"])
            b.v(lambda e: e.tensor_single_scalar(U(6), U(0), 64.0, ALU.is_ge),
                reads=["Uu"], writes=["Umlo"])
            b.v(lambda e, fcid=fcid: e.tensor_mul(HQc(fcid), U(5), U(6)),
                reads=["Usel", "Umlo"], writes=["HQ"])
        for oc in range(NDC):
            pslot = oc % 2
            for fb in range(NFC // 8):
                wslot = fb % 2
                b.dg(lambda e, oc=oc, fb=fb, wslot=wslot: e.dma_start(
                    out=WCH(wslot)[:, 0:1024],
                    in_=din["w2"][oc][:, fb * 1024:(fb + 1) * 1024]),
                    writes=[f"WCH{wslot}"])
                for fi in range(8):
                    fcid = fb * 8 + fi
                    b.t(lambda e, fcid=fcid, fi=fi, wslot=wslot, pslot=pslot: e.matmul(
                        PB[3 + pslot][:, 0:FT], WCH(wslot)[:, fi * 128:(fi + 1) * 128], HQc(fcid),
                        start=(fcid == 0), stop=(fcid == NFC - 1)),
                        reads=[f"WCH{wslot}", "HQ"], writes=[f"PB{3 + pslot}"])
            b.v(lambda e, oc=oc, pslot=pslot: e.tensor_scalar(
                out=U(0), in0=PB[3 + pslot][:, 0:FT], scalar1=256.0,
                scalar2=B2S[:, oc:oc + 1], op0=ALU.mult, op1=ALU.add),
                reads=[f"PB{3 + pslot}", "B2S"], writes=["Uu"])
            b.v(lambda e: e.tensor_scalar(out=U(1), in0=U(0), scalar1=0.5, scalar2=4095.5,
                                          op0=ALU.max, op1=ALU.min),
                reads=["Uu"], writes=["Uc"])
            b.v(lambda e: e.tensor_copy(U(2), U(1)), reads=["Uc"], writes=["Ur"])
            chain(CHAIN4, U(2), "Ur", U(3), "Utmp", {})
            b.v(lambda e: e.tensor_sub(U(1), U(1), U(2)), reads=["Uc", "Ur"], writes=["Uq"])
            b.v(lambda e: e.tensor_scalar(out=U(1), in0=U(1), scalar1=1.0 / 256.0,
                                          scalar2=-8.0, op0=ALU.mult, op1=ALU.add),
                reads=["Uq"], writes=["Uq"])
            b.v(lambda e, oc=oc, tl=tl: e.tensor_add(
                FOUT[:], U(1), ST[:, oc * TOK + tl:oc * TOK + tl + FT]),
                reads=["Uq", "ST"], writes=["FOUT", "Usel"])
            b.dl(lambda e, oc=oc, tl=tl: e.dma_start(
                out=out_t[oc * 128:(oc + 1) * 128, tl:tl + FT], in_=FOUT[:]),
                reads=["FOUT"], writes=["OUTDMA"], force_inc=True)

    for name, shape in debug:
        if name == "s1":
            b.dl(lambda e: e.dma_start(out=dbg_t["s1"], in_=ST[:, 0:NDC * TOK]),
                 reads=["ST"], writes=["DBGDMA"])
        if name == "hq":
            b.dl(lambda e: e.dma_start(out=dbg_t["hq"], in_=AR5[:, 0:NFC * FT]),
                 reads=["HQ"], writes=["DBGDMA"])

    b.v(lambda e: e.memset(YSCR[0:1, 3:4], 0.0), reads=["OUTDMA", "DBGDMA"], writes=[])

    with ExitStack() as st2:
        sems = {}
        for ename in B.ENGS:
            sems[ename] = st2.enter_context(nc.semaphore("sem_" + ename))
        for ename in ("vector", "tensor", "scalar"):
            for i2 in range(2):
                sems[ename + str(i2)] = st2.enter_context(nc.semaphore(f"sem_{ename}{i2}"))
        block = st2.enter_context(nc.Block())
        b.emit_into(block, sems, limit=limit)
    st.close()
    return nc, b


def make_consts():
    tri = np.zeros((L, L), np.float32)
    for s_ in range(L):
        tri[s_, s_:] = 1.0
    maskE = np.where(tri > 0, 0.0, -1.0e30).astype(np.float32)
    ident = np.eye(128, dtype=np.float32)
    return tri, maskE, ident


def make_inmaps(inputs, TOK=2048, PRE=6144):
    x = np.asarray(inputs["L0_attn_x"], np.float32)
    Bb, Tt, _ = x.shape
    tri, maskE, ident = make_consts()
    wif = np.concatenate([np.asarray(inputs["L0_attn_wi"], np.float32),
                          np.asarray(inputs["L0_attn_wf"], np.float32)], axis=1)
    bif = np.concatenate([np.asarray(inputs["L0_attn_bi"], np.float32),
                          np.asarray(inputs["L0_attn_bf"], np.float32)])
    bif_rep = np.tile(bif[None, :], (128, 1)).astype(np.float32)
    def r128(w):
        w = np.asarray(w, np.float32)
        return np.ascontiguousarray(w.reshape(-1, 128, w.shape[1]).transpose(1, 0, 2).reshape(128, -1))
    wg = np.asarray(inputs["L0_attn_w_gate"], np.float32).reshape(D, NGC, 128).transpose(1, 0, 2)
    wg = np.ascontiguousarray(np.stack([r128(wg[g]) for g in range(NGC)]))
    wu = np.asarray(inputs["L0_attn_w_up"], np.float32).reshape(D, NGC, 128).transpose(1, 0, 2)
    wu = np.ascontiguousarray(np.stack([r128(wu[g]) for g in range(NGC)]))
    wd = np.asarray(inputs["L0_attn_w_down"], np.float32).reshape(IG, NDC, 128).transpose(1, 0, 2)
    wd = np.ascontiguousarray(np.stack([r128(wd[o]) for o in range(NDC)]))
    w1 = np.asarray(inputs["L0_ffn_w1"], np.float32).reshape(D, NFC, 128).transpose(1, 0, 2)
    w1 = np.ascontiguousarray(np.stack([r128(w1[f]) for f in range(NFC)]))
    w2 = np.asarray(inputs["L0_ffn_w2"], np.float32).reshape(FF, NDC, 128).transpose(1, 0, 2)
    w2 = np.ascontiguousarray(np.stack([r128(w2[o]) for o in range(NDC)]))
    shared = dict(
        tri=tri, maskE=maskE, ident32=ident,
        wq=r128(np.asarray(inputs["L0_attn_wq"], np.float32)),
        wk=r128(np.asarray(inputs["L0_attn_wk"], np.float32)),
        wv=r128(np.asarray(inputs["L0_attn_wv"], np.float32)),
        wo=r128(np.asarray(inputs["L0_attn_wo"], np.float32)),
        wif=r128(wif), bif_rep=bif_rep,
        ln1=np.ascontiguousarray(np.asarray(inputs["L0_attn_ln1_w"], np.float32).reshape(NDC, 128).T),
        ln2=np.ascontiguousarray(np.asarray(inputs["L0_attn_ln2_w"], np.float32).reshape(NDC, 128).T),
        wg=wg, wu=wu, wd=wd, w1=w1,
        b1=np.ascontiguousarray(np.asarray(inputs["L0_ffn_b1"], np.float32).reshape(NFC, 128).T),
        w2=w2, b2=np.ascontiguousarray(np.asarray(inputs["L0_ffn_b2"], np.float32).reshape(NDC, 128).T),
    )
    n_quart = Tt // TOK
    in_maps = []
    for c in range(8):
        bb = c // n_quart if n_quart < 8 else 0
        bb = c // 4
        qt = c % 4
        xt = np.ascontiguousarray(x[bb].T)
        pre = xt[:, 0:PRE]
        own = xt[:, qt * TOK:(qt + 1) * TOK]
        xbm = np.concatenate([pre, own], axis=1)
        onehot = np.zeros((128, 4), np.float32)
        onehot[:, qt] = 1.0
        m = dict(shared)
        m["xb"] = np.ascontiguousarray(xbm)
        m["onehot"] = onehot
        in_maps.append(m)
    return in_maps


def assemble(results, TOK=2048):
    outs = []
    for bb in range(2):
        quarters = [results[bb * 4 + qt]["out"] for qt in range(4)]
        full = np.concatenate(quarters, axis=1)
        outs.append(full.T)
    return np.stack(outs, axis=0).astype(np.float32)


def kernel(**inputs):
    from concourse.bass_utils import run_bass_kernel_spmd
    nc, b = build_program(TOK=2048, PRE=6144)
    in_maps = make_inmaps(inputs, TOK=2048, PRE=6144)
    res = run_bass_kernel_spmd(nc, in_maps, core_ids=list(range(8)))
    return assemble([r for r in res.results], TOK=2048)
